# version 02
# baseline (speedup 1.0000x reference)
import math
import numpy as np
import ml_dtypes

import concourse.bass as bass
import concourse.mybir as mybir
from concourse.bass_utils import run_bass_kernel_spmd
from contextlib import ExitStack

F32 = mybir.dt.float32
BF16 = mybir.dt.bfloat16
I32 = mybir.dt.int32
AF = mybir.ActivationFunctionType
ALU = mybir.AluOpType
AX = mybir.AxisListType

B, T, C, H, HD = 2, 4096, 4096, 32, 128
R = 8
TOK = 1024
NT = TOK // 128
NCC = C // 128
NOG = C // 512
NER = 256
EPS = 1e-5
HL = 4
NOC = 3 * HL
GT = T * B
NGRP = GT // 512
SMAIN = 1.0 / math.sqrt(HD)
SADAPT = 1.0 / math.sqrt(C)

DEBUG = False
PHASES = "full"


class Sem:
    def __init__(self, es, nc, name):
        self.h = es.enter_context(nc.semaphore(name))
        self.name = name
        self.n = 0

    def inc(self, inst, k=1):
        inst.then_inc(self.h, k)
        self.n += k
        return self.n

    def dinc(self, inst):
        return self.inc(inst, 16)


class Waiter:
    def __init__(self):
        self.hw = {}

    def wait(self, eng, ename, sem, val):
        if val is None or val <= 0:
            return
        key = (ename, sem.name)
        if self.hw.get(key, -1) >= val:
            return
        eng.wait_ge(sem.h, val)
        self.hw[key] = val

    def self_sync(self, eng, ename, sem, inst):
        sem.inc(inst)
        eng.wait_ge(sem.h, sem.n)
        self.hw[(ename, sem.name)] = sem.n


def grp_cols(b, chunk):
    rb = chunk if b == 0 else 7 - chunk
    return rb * 1024 + b * 512


class Builder:
    def __init__(self):
        self.nc = bass.Bass("TRN2", target_bir_lowering=False)
        self.W = Waiter()

    def emit_ln(self, src_ap, with_xa, src_wait_s=0):
        blk, W = self.blk, self.W
        sS, sG, sP, sA, sV = self.sS, self.sG, self.sP, self.sA, self.sV
        zT, zbf, xbuf = self.zT, self.zbf, self.xbuf
        T0, PO = self.T0, self.PO
        st = self.stats

        x_rdy = [0] * NT
        z_done = [0] * NT
        tr_done = [[0] * 4 for _ in range(NT)]
        ev_done = [[0] * 4 for _ in range(NT)]
        xa_mm = [0] * NT
        xa_evac = [0] * NT
        xa_tr = [0] * NT
        self.xaw_done = [0] * NT
        prev_sq_pe = self.ln_prev_zbf_pe

        for t in range(NT):
            def ld(s, t=t):
                W.wait(s, "sy", sS, src_wait_s)
                W.wait(s, "sy", sV, z_done[t - 1] if t >= 1 else self.xbuf_free_v)
                x_rdy[t] = sS.dinc(s.dma_start(
                    out=xbuf[:], in_=src_ap[t * 128:(t + 1) * 128, :]))
            blk.sync(ld)

            def v1(v, t=t):
                W.wait(v, "v", sS, x_rdy[t])
                ins = v.reduce_sum(st["sum"][:], xbuf[:], axis=AX.X)
                W.self_sync(v, "v", sV, ins)
                sV.inc(v.tensor_tensor(st["t1"][:], st["sum"][:], st["sum"][:], op=ALU.mult))
            blk.vector(v1)
            v_t1 = sV.n

            def a1(s, t=t):
                W.wait(s, "a", sS, x_rdy[t])
                W.wait(s, "a", sP, prev_sq_pe[0])
                sA.inc(s.activation(zbf[:], xbuf[:], AF.Square, accum_out=st["ssq"][:]))
            blk.scalar(a1)
            a_ssq = sA.n

            def v2(v, t=t):
                W.wait(v, "v", sA, a_ssq)
                W.wait(v, "v", sV, v_t1)
                ins = v.tensor_scalar(st["tmp"][:], st["t1"][:], -1.0 / (C * C), EPS,
                                      op0=ALU.mult, op1=ALU.add)
                W.self_sync(v, "v", sV, ins)
                sV.inc(v.scalar_tensor_tensor(st["var"][:], st["ssq"][:], 1.0 / C, st["tmp"][:],
                                              op0=ALU.mult, op1=ALU.add))
            blk.vector(v2)
            v_var = sV.n

            def a2(s, t=t):
                W.wait(s, "a", sV, v_var)
                ins = s.activation(st["lnv"][:], st["var"][:], AF.Ln)
                W.self_sync(s, "a", sA, ins)
                sA.inc(s.activation(st["rstd"][:], st["lnv"][:], AF.Exp, scale=-0.5))
            blk.scalar(a2)
            a_rstd = sA.n

            def v3(v, t=t):
                W.wait(v, "v", sA, a_rstd)
                ins = v.scalar_tensor_tensor(st["negmr"][:], st["sum"][:], -1.0 / C, st["rstd"][:],
                                             op0=ALU.mult, op1=ALU.mult)
                W.self_sync(v, "v", sV, ins)
                z_done[t] = sV.inc(v.tensor_scalar(zbf[:], xbuf[:],
                                                   st["rstd"][:], st["negmr"][:],
                                                   op0=ALU.mult, op1=ALU.add))
            blk.vector(v3)

            for bt in range(4):
                def tr(te, t=t, bt=bt):
                    W.wait(te, "t", sV, z_done[t])
                    if bt == 0:
                        prev_ev = ev_done[t - 1][3] if t >= 1 else self.t0_free_v
                        W.wait(te, "t", sA, self.xaw_done[t - 1] if (with_xa and t >= 1)
                               else self.t0_free_a)
                    else:
                        prev_ev = ev_done[t][bt - 1]
                    W.wait(te, "t", sV, prev_ev)
                    for k in range(8):
                        cc = bt * 8 + k
                        ins = te.transpose(T0[:, k * 128:(k + 1) * 128],
                                           zbf[:, cc * 128:(cc + 1) * 128], self.ident[:])
                    tr_done[t][bt] = sP.inc(ins)
                blk.tensor(tr)

                def ev(v, t=t, bt=bt):
                    W.wait(v, "v", sP, tr_done[t][bt])
                    W.wait(v, "v", sP, self.zT_free_pe)
                    for k in range(8):
                        cc = bt * 8 + k
                        ins = v.tensor_copy(zT[:, cc * 1024 + t * 128: cc * 1024 + (t + 1) * 128],
                                            T0[:, k * 128:(k + 1) * 128])
                    ev_done[t][bt] = sV.inc(ins)
                blk.vector(ev)
            prev_sq_pe[0] = tr_done[t][3]

            if with_xa:
                def xam(te, t=t):
                    W.wait(te, "t", sV, ev_done[t][3])
                    W.wait(te, "t", sG, self.const_loaded_g)
                    W.wait(te, "t", sA, xa_evac[t - 1] if t >= 1 else 0)
                    for cc in range(NCC):
                        ins = te.matmul(PO[:, 0:256],
                                        zT[:, cc * 1024 + t * 128: cc * 1024 + (t + 1) * 128],
                                        self.AT[:, cc * 256:(cc + 1) * 256],
                                        start=(cc == 0), stop=(cc == NCC - 1))
                    xa_mm[t] = sP.inc(ins)
                blk.tensor(xam)

                def xae(s, t=t):
                    W.wait(s, "a", sP, xa_mm[t])
                    xa_evac[t] = sA.inc(s.activation(self.xabf[:], PO[:, 0:256], AF.Copy))
                blk.scalar(xae)

                def xat(te, t=t):
                    W.wait(te, "t", sA, xa_evac[t])
                    W.wait(te, "t", sA, self.xaw_done[t - 1] if t >= 1 else 0)
                    W.wait(te, "t", sV, ev_done[t][3])
                    for j in range(2):
                        ins = te.transpose(T0[:, j * 128:(j + 1) * 128],
                                           self.xabf[:, j * 128:(j + 1) * 128], self.ident[:])
                    xa_tr[t] = sP.inc(ins)
                blk.tensor(xat)

                def xaw(s, t=t):
                    W.wait(s, "a", sP, xa_tr[t])
                    W.wait(s, "a", sV, self.wer_ready_v)
                    for j in range(2):
                        ins = s.activation(self.xawT[:, j * 1024 + t * 128: j * 1024 + (t + 1) * 128],
                                           T0[:, j * 128:(j + 1) * 128], AF.Copy,
                                           scale=self.w_er[:, j:j + 1])
                    self.xaw_done[t] = sA.inc(ins)
                blk.scalar(xaw)

        self.xbuf_free_v = z_done[NT - 1]
        self.ln_last_z_evac = ev_done[NT - 1][3]
        self.ln_last_xaw = self.xaw_done[NT - 1] if with_xa else 0
        self.t0_free_v = ev_done[NT - 1][3]
        self.t0_free_a = self.xaw_done[NT - 1] if with_xa else 0

    def emit_mm_stream(self, w_dram, lhs_sb, lora, biasmm, res_scale,
                       xe_src, out_dst, xe_wait_s=0, extra_pe_wait=None):
        blk, W = self.blk, self.W
        sS, sG, sP, sA, sV, sX = self.sS, self.sG, self.sP, self.sA, self.sV, self.sX
        SQ = self.SQ
        wbuf = self.wbuf

        lhs_v = self.ln_last_z_evac
        xaw_a = self.ln_last_xaw if lora else 0
        w_rdy = {}
        mm_done = {}
        wbuf_free = self.wbuf_free
        bank_free = self.sq_bank_free
        xe_rdy = {}
        xe_free = self.xe_free
        ob_free = self.ob_free

        def wload(g, og, hf):
            bi = (og * 2 + hf) % 3
            W.wait(g, "g", sP, wbuf_free[bi])
            src = w_dram[og, hf * 2048:(hf + 1) * 2048, :].rearrange("(cc p) o -> p cc o", p=128)
            dst = wbuf[bi][:].rearrange("p (cc o) -> p cc o", cc=16)
            w_rdy[(og, hf)] = sG.dinc(g.dma_start(out=dst, in_=src))

        blk.gpsimd(lambda g: [wload(g, 0, 0), wload(g, 0, 1)])

        for og in range(NOG):
            if og + 1 < NOG:
                blk.gpsimd(lambda g, og=og: wload(g, og + 1, 0))

            for t in range(NT):
                bank = ((t // 2) % 2) * 2 + (t % 2)
                sl = t % 2

                def xl(s, og=og, t=t, sl=sl):
                    W.wait(s, "x", sS, xe_wait_s)
                    W.wait(s, "x", sV, xe_free[sl])
                    xe_rdy[(og, t)] = sX.dinc(s.dma_start(
                        out=self.xe[sl][:],
                        in_=xe_src[t * 128:(t + 1) * 128, og * 512:(og + 1) * 512]))
                blk.scalar(xl)

                def mm(te, og=og, t=t, bank=bank):
                    W.wait(te, "t", sG, w_rdy[(og, 0)])
                    W.wait(te, "t", sV, lhs_v)
                    if lora:
                        W.wait(te, "t", sA, xaw_a)
                    if extra_pe_wait is not None:
                        extra_pe_wait(te)
                    W.wait(te, "t", sV, bank_free[bank])
                    n_mm = 32 + (2 if lora else 0) + (1 if biasmm else 0)
                    k = 0
                    for hf in range(2):
                        bi = (og * 2 + hf) % 3
                        if hf == 1:
                            W.wait(te, "t", sG, w_rdy[(og, 1)])
                        for cc16 in range(16):
                            cc = hf * 16 + cc16
                            k += 1
                            ins = te.matmul(
                                SQ[:, bank * 512:(bank + 1) * 512],
                                lhs_sb[:, cc * 1024 + t * 128: cc * 1024 + (t + 1) * 128],
                                wbuf[bi][:, cc16 * 512:(cc16 + 1) * 512],
                                start=(k == 1), stop=(k == n_mm))
                    if lora:
                        for j in range(2):
                            k += 1
                            ins = te.matmul(
                                SQ[:, bank * 512:(bank + 1) * 512],
                                self.xawT[:, j * 1024 + t * 128: j * 1024 + (t + 1) * 128],
                                self.BfT[:, j * 4096 + og * 512: j * 4096 + (og + 1) * 512],
                                start=False, stop=(k == n_mm))
                    if biasmm:
                        k += 1
                        ins = te.matmul(SQ[:, bank * 512:(bank + 1) * 512],
                                        self.onesb[0:1, 0:128],
                                        self.bias_bf[0:1, og * 512:(og + 1) * 512],
                                        start=False, stop=True)
                    mm_done[(og, t)] = sP.inc(ins)
                blk.tensor(mm)

                def epi(v, og=og, t=t, bank=bank, sl=sl):
                    W.wait(v, "v", sP, mm_done[(og, t)])
                    W.wait(v, "v", sX, xe_rdy[(og, t)])
                    W.wait(v, "v", sS, ob_free[sl])
                    if res_scale == 1.0:
                        ins = v.tensor_add(self.ob[sl][:], self.xe[sl][:],
                                           SQ[:, bank * 512:(bank + 1) * 512])
                    else:
                        ins = v.scalar_tensor_tensor(self.ob[sl][:],
                                                     SQ[:, bank * 512:(bank + 1) * 512],
                                                     res_scale, self.xe[sl][:],
                                                     op0=ALU.mult, op1=ALU.add)
                    val = sV.inc(ins)
                    bank_free[bank] = val
                    xe_free[sl] = val
                blk.vector(epi)
                epi_v = sV.n

                def ost(s, og=og, t=t, sl=sl, epi_v=epi_v):
                    W.wait(s, "sy", sV, epi_v)
                    ob_free[sl] = sS.dinc(s.dma_start(
                        out=out_dst[t * 128:(t + 1) * 128, og * 512:(og + 1) * 512],
                        in_=self.ob[sl][:]))
                blk.sync(ost)

            wbuf_free[(og * 2) % 3] = mm_done[(og, NT - 1)]
            wbuf_free[(og * 2 + 1) % 3] = mm_done[(og, NT - 1)]
            if og + 1 < NOG:
                blk.gpsimd(lambda g, og=og: wload(g, og + 1, 1))

        self.mm_stream_last_pe = mm_done[(NOG - 1, NT - 1)]
        self.mm_stream_last_store = sS.n

    def emit_z2_gather(self):
        blk, W = self.blk, self.W
        sS, sCC, sV = self.sS, self.sCC, self.sV

        def zst(s):
            W.wait(s, "sy", sV, self.ln_last_z_evac)
            for cc in range(NCC):
                ins = s.dma_start(out=self.z2d_ap[cc * 128:(cc + 1) * 128, :],
                                  in_=self.zT[:, cc * 1024:(cc + 1) * 1024])
                sS.dinc(ins)
        blk.sync(zst)
        z_st = sS.n
        self.z2_export_s = z_st

        def cc(g):
            W.wait(g, "g", sS, z_st)
            sCC.inc(g.collective_compute("AllGather", ALU.bypass,
                                          ins=[self.z2d_ap.opt()], outs=[self.z2g_ap.opt()],
                                          replica_groups=[list(range(R))]))
        blk.gpsimd(cc)
        self.cc_z_done = sCC.n

    def emit_qkv_hs(self):
        blk, W = self.blk, self.W
        sS, sG, sP, sA, sV = self.sS, self.sG, self.sP, self.sA, self.sV
        SQ, T0 = self.SQ, self.T0

        w_rdy = {}
        wq_free = [0, 0]
        zg_rdy = {}
        zg_free = [0, 0]
        mm_done = {}
        bank_free_v = [0] * 4
        bank_free_a = [0] * 4
        robf_free = [0, 0]
        vst = {"tr": 0, "ev": 0, "st": 0, "vbf": 0}

        def wload(g, grp, oc):
            sl = (grp * NOC + oc) % 2
            W.wait(g, "g", sP, wq_free[sl])
            src = self.wqkv_blk[oc].rearrange("(cc p) o -> p cc o", p=128)
            dst = self.wqb[sl][:].rearrange("p (cc o) -> p cc o", cc=32)
            w_rdy[(grp, oc)] = sG.dinc(g.dma_start(out=dst, in_=src))

        def zload(g, grp):
            sl = grp % 2
            W.wait(g, "g", sP, zg_free[sl])
            W.wait(g, "g", self.sCC, self.cc_z_done)
            rb, tl = grp // 2, grp % 2
            src = self.z2g_ap[rb * 4096:(rb + 1) * 4096,
                              tl * 512:(tl + 1) * 512].rearrange("(cc p) t -> p cc t", p=128)
            dst = self.zgb[sl][:].rearrange("p (cc t) -> p cc t", cc=32)
            zg_rdy[grp] = sG.dinc(g.dma_start(out=dst, in_=src))

        blk.gpsimd(lambda g: [zload(g, 0), wload(g, 0, 0), wload(g, 0, 1)])

        for grp in range(NGRP):
            rb, tl = grp // 2, grp % 2
            chunk = rb if tl == 0 else 7 - rb
            poscol = chunk * 512
            dstcols = rb * 1024 + tl * 512
            if grp + 1 < NGRP:
                blk.gpsimd(lambda g, grp=grp: zload(g, grp + 1))

            for oc in range(NOC):
                zsl = grp % 2
                wsl = (grp * NOC + oc) % 2
                bank = (grp * NOC + oc) % 4
                ng, no = (grp, oc + 1) if oc + 1 < NOC else (grp + 1, 0)
                if ng < NGRP:
                    blk.gpsimd(lambda g, ng=ng, no=no: wload(g, ng, no))

                def mm(te, grp=grp, oc=oc, zsl=zsl, wsl=wsl, bank=bank):
                    W.wait(te, "t", sG, w_rdy[(grp, oc)])
                    W.wait(te, "t", sG, zg_rdy[grp])
                    W.wait(te, "t", sV, bank_free_v[bank])
                    W.wait(te, "t", sA, bank_free_a[bank])
                    for cc in range(NCC):
                        ins = te.matmul(SQ[:, bank * 512:(bank + 1) * 512],
                                        self.wqb[wsl][:, cc * 128:(cc + 1) * 128],
                                        self.zgb[zsl][:, cc * 512:(cc + 1) * 512],
                                        start=(cc == 0), stop=(cc == NCC - 1))
                    mm_done[(grp, oc)] = sP.inc(ins)
                    wq_free[wsl] = mm_done[(grp, oc)]
                blk.tensor(mm)

                hl = oc % HL
                if oc < 8:
                    sl = (grp * NOC + oc) % 2

                    def rope(v, grp=grp, oc=oc, bank=bank, sl=sl, poscol=poscol):
                        W.wait(v, "v", sP, mm_done[(grp, oc)])
                        W.wait(v, "v", sS, robf_free[sl])
                        W.wait(v, "v", sG, self.rope_tables_g)
                        bk = SQ[:, bank * 512:(bank + 1) * 512]
                        v.tensor_tensor(self.rt1[:], bk, self.cosf[:, poscol:poscol + 512],
                                        op=ALU.mult)
                        v.tensor_tensor(self.rt2[0:64, :], bk[64:128, :],
                                        self.sinf[0:64, poscol:poscol + 512], op=ALU.mult)
                        v.tensor_tensor(self.rt2[64:128, :], bk[0:64, :],
                                        self.sinf[64:128, poscol:poscol + 512], op=ALU.mult)
                        ins = v.tensor_add(self.robf[sl][:], self.rt1[:], self.rt2[:])
                        bank_free_v[bank] = sV.inc(ins)
                    blk.vector(rope)
                    rope_v = sV.n

                    def rst(s, oc=oc, hl=hl, sl=sl, rope_v=rope_v, dstcols=dstcols):
                        W.wait(s, "sy", sV, rope_v)
                        dst = self.qdl_ap if oc < HL else self.kd_ap
                        robf_free[sl] = sS.dinc(s.dma_start(
                            out=dst[hl * 128:(hl + 1) * 128, dstcols:dstcols + 512],
                            in_=self.robf[sl][:]))
                    blk.sync(rst)
                else:
                    def vc(s, grp=grp, oc=oc, bank=bank):
                        W.wait(s, "a", sP, mm_done[(grp, oc)])
                        W.wait(s, "a", sP, vst["tr"])
                        bank_free_a[bank] = sA.inc(
                            s.activation(self.vbf[:], SQ[:, bank * 512:(bank + 1) * 512],
                                         AF.Copy))
                        vst["vbf"] = bank_free_a[bank]
                    blk.scalar(vc)

                    def vt(te):
                        W.wait(te, "t", sA, vst["vbf"])
                        W.wait(te, "t", sV, vst["ev"] if vst["ev"] else self.t0_free_v)
                        W.wait(te, "t", sA, self.t0_free_a)
                        for j in range(4):
                            ins = te.transpose(T0[:, j * 128:(j + 1) * 128],
                                               self.vbf[:, j * 128:(j + 1) * 128], self.ident[:])
                        vst["tr"] = sP.inc(ins)
                    blk.tensor(vt)

                    def vev(v):
                        W.wait(v, "v", sP, vst["tr"])
                        W.wait(v, "v", sS, vst["st"])
                        for j in range(4):
                            ins = v.tensor_copy(self.vtm[:, j * 128:(j + 1) * 128],
                                                T0[:, j * 128:(j + 1) * 128])
                        vst["ev"] = sV.inc(ins)
                    blk.vector(vev)

                    def vsto(s, hl=hl, dstcols=dstcols):
                        W.wait(s, "sy", sV, vst["ev"])
                        for j in range(4):
                            ins = s.dma_start(
                                out=self.vd_ap[dstcols + j * 128: dstcols + (j + 1) * 128,
                                               hl * 128:(hl + 1) * 128],
                                in_=self.vtm[:, j * 128:(j + 1) * 128])
                            vst["st"] = sS.dinc(ins)
                    blk.sync(vsto)

            zg_free[grp % 2] = mm_done[(grp, NOC - 1)]

        self.qkv_last_pe = mm_done[(NGRP - 1, NOC - 1)]
        self.t0_free_v = vst["ev"]

    def emit_adapter(self):
        blk, W = self.blk, self.W
        sS, sG, sP, sA, sV = self.sS, self.sG, self.sP, self.sA, self.sV
        PA, T0 = self.PA, self.T0
        ad = {"aw": 0, "mm": 0, "ev": 0, "tr": 0}

        for part in range(2):
            def awl(g, part=part):
                W.wait(g, "g", sP, self.qkv_last_pe if part == 0 else ad["mm"])
                for i in range(4):
                    oc = 4 + part * 4 + i
                    src = self.wqkv_blk[oc].rearrange("(cc p) o -> p cc o", p=128)
                    dst = self.zgb[0][:].rearrange("p (cc q) -> p cc q", cc=32)[
                        :, :, i * 128:(i + 1) * 128]
                    ad["aw"] = sG.dinc(g.dma_start(out=dst, in_=src))
            blk.gpsimd(awl)

            def amm(te, part=part):
                W.wait(te, "t", sG, ad["aw"])
                W.wait(te, "t", sG, self.pref_loaded_g)
                if part == 1:
                    W.wait(te, "t", sA, ad["ev"])
                    W.wait(te, "t", sP, ad["tr"])
                for cc in range(NCC):
                    ins = te.matmul(PA[0:10, :], self.prefT[:, cc * 10:(cc + 1) * 10],
                                    self.zgb[0][:, cc * 512:(cc + 1) * 512],
                                    start=(cc == 0), stop=(cc == NCC - 1))
                ad["mm"] = sP.inc(ins)
            blk.tensor(amm)

            def aev(s, part=part):
                W.wait(s, "a", sP, ad["mm"])
                ad["ev"] = sA.inc(s.activation(self.aksb[:], PA[0:10, :], AF.Copy))
            blk.scalar(aev)

            if part == 0:
                def atr(te):
                    W.wait(te, "t", sA, ad["ev"])
                    W.wait(te, "t", sV, self.t0_free_v)
                    W.wait(te, "t", sA, self.t0_free_a)
                    for j in range(4):
                        ins = te.transpose(T0[:, j * 128: j * 128 + 10],
                                           self.aksb[0:10, j * 128:(j + 1) * 128],
                                           self.ident[0:10, 0:10])
                    ad["tr"] = sP.inc(ins)
                blk.tensor(atr)

                def aevt(v):
                    W.wait(v, "v", sP, ad["tr"])
                    for j in range(4):
                        ins = v.tensor_copy(self.akt[:, j * 10:(j + 1) * 10],
                                            T0[:, j * 128: j * 128 + 10])
                    self.ak_ready_v = sV.inc(ins)
                    self.t0_free_v = self.ak_ready_v
                blk.vector(aevt)
            else:
                def avc(v):
                    W.wait(v, "v", sA, ad["ev"])
                    self.av_ready_v = sV.inc(v.tensor_copy(self.avt[:], self.aksb[:]))
                blk.vector(avc)
        self.adapter_pe = ad["mm"]
        self.qkv_spill_s = self.sS.n

    def emit_attention_hs(self):
        blk, W = self.blk, self.W
        sS, sG, sP, sA, sV = self.sS, self.sG, self.sP, self.sA, self.sV
        SQ, PO, PL, PA = self.SQ, self.PO, self.PL, self.PA
        kt, vt, qt = self.ktile, self.vtile, self.qtile
        E = self.Etile

        units = [(b, hl, qi) for b in range(B) for hl in range(HL) for qi in range(8)]
        tasks = []
        for ui, (b, hl, qi) in enumerate(units):
            for kb in range(qi + 1):
                tasks.append((ui, kb))
        tasknum = {t: i for i, t in enumerate(tasks)}

        ld_rdy = {}
        q_rdy = {}
        qk_done = {}
        exp_done = {}
        msk_done = {}
        pv_done = {}
        task_prev_pe = {}
        unit_q_free = {}
        u = {"nrm_v": 0, "nbe_a": 0, "nil_a": 0, "ysto": 0, "exp_prev": 0,
             "sa": 0, "ea": 0, "la": 0, "ga": 0, "gab": 0, "bc": 0, "ean": 0,
             "apv": 0, "nb": 0}

        def tload(s, ui, kb):
            b, hl, qi = units[ui]
            sl = tasknum[(ui, kb)] % 3
            W.wait(s, "sy", sP, task_prev_pe.get((ui, kb), 0))
            cols = grp_cols(b, kb)
            sS.dinc(s.dma_start(out=kt[sl][:],
                                in_=self.kd_ap[hl * 128:(hl + 1) * 128, cols:cols + 512]))
            ld = s.dma_start(out=vt[sl][:].rearrange("p (c d) -> p c d", c=4),
                             in_=self.vd_ap[cols:cols + 512,
                                            hl * 128:(hl + 1) * 128].rearrange(
                                 "(c p) d -> p c d", p=128))
            ld_rdy[(ui, kb)] = sS.dinc(ld)

        def qload(s, ui):
            b, hl, qi = units[ui]
            W.wait(s, "sy", sP, unit_q_free.get(ui, 0))
            cols = grp_cols(b, qi)
            q_rdy[ui] = sS.dinc(s.dma_start(
                out=qt[ui % 2][:],
                in_=self.qdl_ap[hl * 128:(hl + 1) * 128, cols:cols + 512]))

        def first(s):
            W.wait(s, "sy", sP, self.adapter_pe)
            W.wait(s, "sy", sV, self.av_ready_v)
            qload(s, 0)
            tload(s, 0, 0)
        blk.sync(first)

        for ui, (b, hl, qi) in enumerate(units):
            if ui + 1 < len(units):
                blk.sync(lambda s, ui=ui: qload(s, ui + 1))

            for kb in range(qi + 1):
                tn = tasknum[(ui, kb)]
                sl = tn % 3
                esl = tn % 2
                if tn + 1 < len(tasks):
                    nui, nkb = tasks[tn + 1]
                    if tn - 2 >= 0:
                        task_prev_pe[tasks[tn + 1]] = pv_done.get(tasks[tn - 2], 0)
                    blk.sync(lambda s, nui=nui, nkb=nkb: tload(s, nui, nkb))

                def qk(te, ui=ui, kb=kb, sl=sl):
                    W.wait(te, "t", sS, ld_rdy[(ui, kb)])
                    W.wait(te, "t", sS, q_rdy[ui])
                    W.wait(te, "t", sA, u["exp_prev"])
                    for c in range(4):
                        ins = te.matmul(SQ[:, c * 512:(c + 1) * 512],
                                        kt[sl][:, c * 128:(c + 1) * 128], qt[ui % 2][:],
                                        start=True, stop=True)
                    qk_done[(ui, kb)] = sP.inc(ins)
                blk.tensor(qk)

                def ex(s, ui=ui, kb=kb, esl=esl, tn=tn):
                    W.wait(s, "a", sP, qk_done[(ui, kb)])
                    if tn >= 2:
                        W.wait(s, "a", sP, pv_done.get(tasks[tn - 2], 0))
                    exp_done[(ui, kb)] = sA.inc(
                        s.activation(E[esl][:], SQ[:, 0:2048], AF.Exp, scale=SMAIN))
                    u["exp_prev"] = exp_done[(ui, kb)]
                blk.scalar(ex)

                if kb == qi:
                    def mk(v, ui=ui, kb=kb, esl=esl):
                        W.wait(v, "v", sA, exp_done[(ui, kb)])
                        msk_done[(ui, kb)] = sV.inc(
                            v.tensor_tensor(E[esl][:], E[esl][:], self.dmask[:], op=ALU.mult))
                    blk.vector(mk)

                def pv(te, ui=ui, kb=kb, sl=sl, esl=esl, qi=qi):
                    W.wait(te, "t", sA, exp_done[(ui, kb)])
                    if kb == qi:
                        W.wait(te, "t", sV, msk_done[(ui, kb)])
                    if kb == 0:
                        W.wait(te, "t", sV, u["nrm_v"])
                        W.wait(te, "t", sA, u["nil_a"])
                    for c in range(4):
                        te.matmul(PO[:, :], vt[sl][:, c * 128:(c + 1) * 128],
                                  E[esl][:, c * 512:(c + 1) * 512],
                                  start=(kb == 0 and c == 0), stop=False)
                    for c in range(4):
                        ins = te.matmul(PL[0:1, :], self.onesb[:, 0:1],
                                        E[esl][:, c * 512:(c + 1) * 512],
                                        start=(kb == 0 and c == 0),
                                        stop=(kb == qi and c == 3))
                    pv_done[(ui, kb)] = sP.inc(ins)
                blk.tensor(pv)

            def asa(te, ui=ui, hl=hl):
                W.wait(te, "t", sV, self.ak_ready_v)
                W.wait(te, "t", sV, self.av_ready_v)
                W.wait(te, "t", sA, u["nbe_a"])
                ins = te.matmul(PA[0:10, :], self.akt[:, hl * 10:(hl + 1) * 10],
                                qt[ui % 2][:], start=True, stop=True)
                u["sa"] = sP.inc(ins)
            blk.tensor(asa)

            def aex(s, ui=ui):
                W.wait(s, "a", sP, u["sa"])
                u["ea"] = sA.inc(s.activation(self.easb[:], PA[0:10, :], AF.Exp,
                                              scale=SADAPT))
            blk.scalar(aex)

            def ala(te, ui=ui):
                W.wait(te, "t", sA, u["ea"])
                u["la"] = sP.inc(te.matmul(PA[32:33, :], self.onesb[0:10, 0:1],
                                           self.easb[:], start=True, stop=True))
            blk.tensor(ala)

            def nil(s, ui=ui, qi=qi):
                W.wait(s, "a", sP, pv_done[(ui, qi)])
                ins = s.activation(self.lml[:], PL[0:1, :], AF.Ln)
                W.self_sync(s, "a", sA, ins)
                u["nil_a"] = sA.inc(s.activation(self.il_bf[:], self.lml[:],
                                                 AF.Exp, scale=-1.0))
            blk.scalar(nil)

            def aln(s, ui=ui):
                W.wait(s, "a", sP, u["la"])
                u["lna"] = sA.inc(s.activation(self.lln[:], PA[32:33, :], AF.Ln))
            blk.scalar(aln)

            def gdf(v, ui=ui):
                W.wait(v, "v", sA, u["lna"])
                u["gdv"] = sV.inc(v.tensor_tensor(self.gdf1[:], self.lml[:], self.lln[:],
                                                  op=ALU.subtract))
            blk.vector(gdf)

            def gex(s, ui=ui):
                W.wait(s, "a", sV, u["gdv"])
                u["ga"] = sA.inc(s.activation(self.ga1[:], self.gdf1[:], AF.Exp))
            blk.scalar(gex)

            def agv(v, ui=ui):
                W.wait(v, "v", sA, u["ga"])
                u["gab"] = sV.inc(v.tensor_single_scalar(
                    self.ga_bf[:], self.ga1[:], self.gating_sb[0:1, 0:1], op=ALU.mult))
            blk.vector(agv)

            def abc(te, ui=ui):
                W.wait(te, "t", sV, u["gab"])
                u["bc"] = sP.inc(te.matmul(PA[0:10, :], self.onesb[0:1, 0:10],
                                           self.ga_bf[:], start=True, stop=True))
            blk.tensor(abc)

            def aen(v, ui=ui):
                W.wait(v, "v", sP, u["bc"])
                u["ean"] = sV.inc(v.tensor_tensor(self.ean_bf[:], self.easb[:],
                                                  PA[0:10, :], op=ALU.mult))
            blk.vector(aen)

            def apv(te, ui=ui, hl=hl):
                W.wait(te, "t", sV, u["ean"])
                ins = te.matmul(PO[:, :], self.avt[0:10, hl * 128:(hl + 1) * 128],
                                self.ean_bf[:], start=False, stop=True)
                u["apv"] = sP.inc(ins)
            blk.tensor(apv)
            unit_q_free[ui + 2] = u["apv"]

            def nbm(te, ui=ui):
                W.wait(te, "t", sA, u["nil_a"])
                u["nb"] = sP.inc(te.matmul(PA[:, :], self.onesb[0:1, 0:128],
                                           self.il_bf[:], start=True, stop=True))
            blk.tensor(nbm)

            def nbe(s, ui=ui):
                W.wait(s, "a", sP, u["nb"])
                W.wait(s, "a", sV, u["nrm_v"])
                u["nbe_a"] = sA.inc(s.activation(self.nb_sb[:], PA[:, :], AF.Copy))
            blk.scalar(nbe)

            def nmu(v, ui=ui):
                W.wait(v, "v", sA, u["nbe_a"])
                W.wait(v, "v", sP, u["apv"])
                W.wait(v, "v", sS, u["ysto"])
                u["nrm_v"] = sV.inc(v.tensor_tensor(self.ytile[:], PO[:, :],
                                                    self.nb_sb[:], op=ALU.mult))
            blk.vector(nmu)

            def yst(s, ui=ui, b=b, hl=hl, qi=qi):
                W.wait(s, "sy", sV, u["nrm_v"])
                cols = grp_cols(b, qi)
                u["ysto"] = sS.dinc(s.dma_start(
                    out=self.yl_ap[hl * 128:(hl + 1) * 128, cols:cols + 512],
                    in_=self.ytile[:]))
            blk.sync(yst)

        self.attn_last_s = sS.n
        self.attn_last_v = sV.n

    def emit_proj_hs(self):
        blk, W = self.blk, self.W
        sS, sG, sP, sA, sV = self.sS, self.sG, self.sP, self.sA, self.sV
        SQ = self.SQ

        y_rdy = {}
        mm_done = {}
        ev_done = {}
        st_done = {}
        bank_free = [0] * 4
        ob_free = [0, 0]
        ycb_free = [0, 0, 0]

        def wpl(g):
            W.wait(g, "g", sS, self.attn_last_s)
            W.wait(g, "g", sV, self.attn_last_v)
            src = self.wp_my.rearrange("(hl p) o -> p hl o", p=128)
            dst = self.wpb[:].rearrange("p (hl o) -> p hl o", hl=HL)
            sG.dinc(g.dma_start(out=dst, in_=src))
        blk.gpsimd(wpl)
        wp_rdy = self.sG.n

        def yld(s, t):
            sl = t % 3
            W.wait(s, "sy", sS, self.attn_last_s)
            W.wait(s, "sy", sP, ycb_free[sl])
            src = self.yl_ap[:, t * 128:(t + 1) * 128].rearrange("(hl p) t -> p hl t", p=128)
            dst = self.ycb[sl][:].rearrange("p (hl t) -> p hl t", hl=HL)
            y_rdy[t] = sS.dinc(s.dma_start(out=dst, in_=src))

        blk.sync(lambda s: (yld(s, 0), yld(s, 1)))

        NTG = GT // 128
        for t in range(NTG):
            sl = t % 3
            if t + 2 < NTG:
                blk.sync(lambda s, t=t: yld(s, t + 2))

            for og in range(NOG):
                bank = (t * NOG + og) % 4
                osl = (t * NOG + og) % 2

                def mm(te, t=t, og=og, sl=sl, bank=bank):
                    W.wait(te, "t", sS, y_rdy[t])
                    W.wait(te, "t", sG, wp_rdy)
                    W.wait(te, "t", sA, bank_free[bank])
                    for hl in range(HL):
                        ins = te.matmul(
                            SQ[:, bank * 512:(bank + 1) * 512],
                            self.ycb[sl][:, hl * 128:(hl + 1) * 128],
                            self.wpb[:, hl * 4096 + og * 512: hl * 4096 + (og + 1) * 512],
                            start=(hl == 0), stop=(hl == HL - 1))
                    mm_done[(t, og)] = sP.inc(ins)
                    if og == NOG - 1:
                        ycb_free[sl] = mm_done[(t, og)]
                blk.tensor(mm)

                def ev(s, t=t, og=og, bank=bank, osl=osl):
                    W.wait(s, "a", sP, mm_done[(t, og)])
                    W.wait(s, "a", sS, ob_free[osl])
                    ins = s.activation(self.pob[osl][:],
                                       SQ[:, bank * 512:(bank + 1) * 512], AF.Copy)
                    ev_done[(t, og)] = sA.inc(ins)
                    bank_free[bank] = ev_done[(t, og)]
                blk.scalar(ev)

                def pst(s, t=t, og=og, osl=osl):
                    W.wait(s, "sy", sA, ev_done[(t, og)])
                    ob_free[osl] = sS.dinc(s.dma_start(
                        out=self.prtl_ap[og // 2][t * 128:(t + 1) * 128,
                                                  (og % 2) * 512:(og % 2) * 512 + 512],
                        in_=self.pob[osl][:]))
                    st_done[(t, og)] = sS.n
                blk.sync(pst)

        self.rs_done = [0] * 4

        def rs(g):
            for p in range(4):
                W.wait(g, "g", sS, st_done[(NTG - 1, 2 * p + 1)])
                ins = g.collective_compute(
                    "ReduceScatter", ALU.add,
                    ins=[self.prtl_ap[p].opt()],
                    outs=[self.rso_ap[p].opt()],
                    replica_groups=[list(range(R))])
                self.rs_done[p] = self.sCC.inc(ins)
        blk.gpsimd(rs)

        rb_free = [0, 0]
        xb_free = [0, 0]
        self._y2st = {}
        for p in range(4):
            for t in range(NT):
                sl = (p * NT + t) % 2

                def l1(s, p=p, t=t, sl=sl):
                    W.wait(s, "sy", self.sCC, self.rs_done[p])
                    W.wait(s, "sy", sV, xb_free[sl])
                    sS.dinc(s.dma_start(
                        out=self.xe[sl][:],
                        in_=self.y1d[t * 128:(t + 1) * 128, p * 1024:p * 1024 + 512]))
                    self._y2l1 = sS.dinc(s.dma_start(
                        out=self.xe2[sl][:],
                        in_=self.y1d[t * 128:(t + 1) * 128, p * 1024 + 512:(p + 1) * 1024]))
                blk.sync(l1)

                def l2(g, p=p, t=t, sl=sl):
                    W.wait(g, "g", self.sCC, self.rs_done[p])
                    W.wait(g, "g", sV, rb_free[sl])
                    self._y2l2 = sG.dinc(g.dma_start(
                        out=self.rsb[sl][:],
                        in_=self.rso_ap[p][t * 128:(t + 1) * 128, :]))
                blk.gpsimd(l2)

                def add(v, p=p, t=t, sl=sl):
                    W.wait(v, "v", sS, self._y2l1)
                    W.wait(v, "v", sG, self._y2l2)
                    W.wait(v, "v", sS, self._y2st.get((p * NT + t - 2), 0))
                    v.tensor_add(self.ob[sl][:], self.xe[sl][:], self.rsb[sl][:, 0:512])
                    ins = v.tensor_add(self.ob2[sl][:], self.xe2[sl][:],
                                       self.rsb[sl][:, 512:1024])
                    val = sV.inc(ins)
                    xb_free[sl] = val
                    rb_free[sl] = val
                    self._y2add = val
                blk.vector(add)

                def sto(s, p=p, t=t, sl=sl):
                    W.wait(s, "sy", sV, self._y2add)
                    sS.dinc(s.dma_start(
                        out=self.y2d[t * 128:(t + 1) * 128, p * 1024:p * 1024 + 512],
                        in_=self.ob[sl][:]))
                    self._y2st[p * NT + t] = sS.dinc(s.dma_start(
                        out=self.y2d[t * 128:(t + 1) * 128, p * 1024 + 512:(p + 1) * 1024],
                        in_=self.ob2[sl][:]))
                blk.sync(sto)

        self.y2_store_s = sS.n
        self.y2_add_v = sV.n

    def _trunc(self, es, outer_es):
        nc, blk = self.nc, self.blk
        tz = es.enter_context(nc.sbuf_tensor("tzfin", [128, 512], F32))

        def cp(g):
            g.memset(tz[:], 0.0)
            for t in range(NT):
                for cq in range(C // 512):
                    self.sG.dinc(g.dma_start(
                        out=self.out[t * 128:(t + 1) * 128, cq * 512:(cq + 1) * 512],
                        in_=tz[:]))
        blk.gpsimd(cp)

        def fin(s):
            for sem in [self.sS, self.sG, self.sP, self.sA, self.sV, self.sX, self.sCC]:
                if sem.n > 0:
                    s.wait_ge(sem.h, sem.n)
        blk.sync(fin)
        es.close()
        self.blk = None
        outer_es.close()
        return nc

    def build(self):
        nc = self.nc
        es = ExitStack()

        def din(name, shape, dt):
            return nc.dram_tensor(name, shape, dt, kind="ExternalInput").ap()

        self.xin = din("xin", [TOK, C], F32)
        self.wqkv_blk = din("wqkv_blk", [NOC, C, 128], F32)
        self.w_blk = din("w_blk", [NOG, C, 512], F32)
        self.wp_my = din("wp_my", [HL * 128, C], F32)
        self.a_t = din("a_t", [C, NER], F32)
        self.bf_t = din("bf_t", [NER, C], F32)
        self.bias_in = din("bias_in", [1, C], F32)
        self.coeffs_in = din("coeffs_in", [16, 1], F32)
        self.scales_in = din("scales_in", [16, 1], F32)
        self.sel_in = din("sel_in", [16, 256], F32)
        self.pref_in = din("pref_in", [C, 10], F32)
        self.gating_in = din("gating_in", [1, 1], F32)
        self.cos_in = din("cos_in", [128, T], F32)
        self.sin_in = din("sin_in", [128, T], F32)
        self.ident_in = din("ident_in", [128, 128], BF16)
        self.ones_in = din("ones_in", [128, 128], BF16)
        self.dmask_in = din("dmask_in", [128, 2048], BF16)

        self.out = nc.dram_tensor("out", [TOK, C], F32, kind="ExternalOutput").ap()

        dkind = "ExternalOutput" if DEBUG else "Internal"
        self.y1d = nc.dram_tensor("y1d", [TOK, C], F32, kind=dkind).ap()
        self.y2d = nc.dram_tensor("y2d", [TOK, C], F32, kind=dkind).ap()
        self.z2d_ap = nc.dram_tensor("z2d", [C, TOK], BF16, kind="Internal").ap()
        self.z2g_ap = nc.dram_tensor("z2g", [R * C, TOK], BF16, kind="Internal").ap()
        self.kd_ap = nc.dram_tensor("kd", [HL * 128, GT], BF16, kind=dkind).ap()
        self.vd_ap = nc.dram_tensor("vd", [GT, HL * 128], BF16, kind=dkind).ap()
        self.qdl_ap = nc.dram_tensor("qdl", [HL * 128, GT], BF16, kind=dkind).ap()
        self.yl_ap = nc.dram_tensor("yl", [HL * 128, GT], BF16, kind=dkind).ap()
        self.prtl_ap = [nc.dram_tensor(f"prtl{p}", [GT, 1024], BF16, kind="Internal").ap()
                        for p in range(4)]
        self.rso_ap = [nc.dram_tensor(f"rso{p}", [TOK, 1024], BF16, kind="Internal").ap()
                       for p in range(4)]
        if DEBUG:
            self.rsod = nc.dram_tensor("rsod", [4 * TOK, 1024], BF16,
                                       kind="ExternalOutput").ap()

        e = es.enter_context
        self.SQ = e(nc.psum_tensor("SQ", [128, 2048], F32))
        self.PO = e(nc.psum_tensor("PO", [128, 512], F32))
        self.PL = e(nc.psum_tensor("PL", [128, 512], F32))
        self.PA = e(nc.psum_tensor("PA", [128, 512], F32))
        self.T0 = e(nc.psum_tensor("T0", [128, 1024], BF16))

        self.sS = Sem(es, nc, "sS")
        self.sG = Sem(es, nc, "sG")
        self.sP = Sem(es, nc, "sP")
        self.sA = Sem(es, nc, "sA")
        self.sV = Sem(es, nc, "sV")
        self.sX = Sem(es, nc, "sX")
        self.sCC = Sem(es, nc, "sCC")
        sS, sG, sP, sA, sV = self.sS, self.sG, self.sP, self.sA, self.sV
        W = self.W

        self.ident = e(nc.sbuf_tensor("ident", [128, 128], BF16))
        self.onesb = e(nc.sbuf_tensor("onesb", [128, 128], BF16))
        self.bias_bf = e(nc.sbuf_tensor("bias_bf", [1, C], BF16))
        self.w_er = e(nc.sbuf_tensor("w_er", [128, 2], F32))
        self.coeffs_sb = e(nc.sbuf_tensor("coeffs_sb", [16, 1], F32))
        self.scales_sb = e(nc.sbuf_tensor("scales_sb", [16, 1], F32))
        self.w16 = e(nc.sbuf_tensor("w16", [16, 1], F32))
        self.sel_sb = e(nc.sbuf_tensor("sel_sb", [16, 256], F32))
        self.gating_sb = e(nc.sbuf_tensor("gating_sb", [1, 1], F32))
        self.dmask = e(nc.sbuf_tensor("dmask", [128, 2048], BF16))
        self.akt = e(nc.sbuf_tensor("akt", [128, 40], BF16))
        self.avt = e(nc.sbuf_tensor("avt", [10, 512], BF16))
        self.stats = {k: e(nc.sbuf_tensor(f"st_{k}", [128, 1], F32))
                      for k in ["sum", "ssq", "t1", "tmp", "var", "lnv", "rstd", "negmr"]}

        blk = es.enter_context(nc.Block())
        self.blk = blk

        self.xbuf_free_v = 0
        self.zT_free_pe = 0
        self.t0_free_v = 0
        self.t0_free_a = 0
        self.ln_prev_zbf_pe = [0, 0]
        self.rope_tables_g = 0
        self.pref_loaded_g = 0

        def setup_s(s):
            sS.dinc(s.dma_start(out=self.ident[:], in_=self.ident_in))
            sS.dinc(s.dma_start(out=self.onesb[:], in_=self.ones_in))
            sS.dinc(s.dma_start(out=self.coeffs_sb[:], in_=self.coeffs_in))
            sS.dinc(s.dma_start(out=self.scales_sb[:], in_=self.scales_in))
            sS.dinc(s.dma_start(out=self.sel_sb[:], in_=self.sel_in))
            sS.dinc(s.dma_start(out=self.gating_sb[:], in_=self.gating_in))
            sS.dinc(s.dma_start(out=self.dmask[:], in_=self.dmask_in))
        blk.sync(setup_s)
        self.setup_s_n = sS.n

        def setup_g(g):
            sG.dinc(g.dma_start(out=self.bias_bf[:], in_=self.bias_in))
        blk.gpsimd(setup_g)

        def werv(v):
            W.wait(v, "v", sS, self.setup_s_n)
            sV.inc(v.tensor_tensor(self.w16[:], self.coeffs_sb[:], self.scales_sb[:],
                                   op=ALU.mult))
        blk.vector(werv)
        wer_v = sV.n

        def werm(t):
            W.wait(t, "t", sV, wer_v)
            t.matmul(self.PA[:, 0:1], self.sel_sb[0:16, 0:128], self.w16[:],
                     start=True, stop=True)
            sP.inc(t.matmul(self.PA[:, 1:2], self.sel_sb[0:16, 128:256], self.w16[:],
                            start=True, stop=True))
        blk.tensor(werm)
        wer_p = sP.n

        def werc(v):
            W.wait(v, "v", sP, wer_p)
            sV.inc(v.tensor_copy(self.w_er[:], self.PA[:, 0:2]))
        blk.vector(werc)
        self.wer_ready_v = sV.n

        scA = ExitStack()
        eA = scA.enter_context
        zT = eA(nc.sbuf_tensor("zT", [128, NCC * 1024], BF16))
        self.xbuf = eA(nc.sbuf_tensor("xbuf", [128, C], F32))
        self.zbf = eA(nc.sbuf_tensor("zbf0", [128, C], BF16))
        self.wbuf = [eA(nc.sbuf_tensor(f"wb{i}", [128, 8192], BF16)) for i in range(3)]
        self.xawT = eA(nc.sbuf_tensor("xawT", [128, 2048], BF16))
        self.xabf = eA(nc.sbuf_tensor("xabf", [128, 256], BF16))
        self.xe = [eA(nc.sbuf_tensor(f"xe{i}", [128, 512], F32)) for i in range(2)]
        self.ob = [eA(nc.sbuf_tensor(f"ob{i}", [128, 512], F32)) for i in range(2)]
        self.zT = zT
        self.wbuf_free = [0, 0, 0]
        self.sq_bank_free = [0, 0, 0, 0]
        self.xe_free = [0, 0]
        self.ob_free = [0, 0]

        scA1 = ExitStack()
        eA1 = scA1.enter_context
        AT = eA1(nc.sbuf_tensor("AT", [128, NCC * 256], BF16))
        BfT = eA1(nc.sbuf_tensor("BfT", [128, 2 * C], BF16))
        self.AT, self.BfT = AT, BfT

        def cload(g):
            sG.dinc(g.dma_start(out=AT[:].rearrange("p (cc e) -> p cc e", cc=NCC),
                                in_=self.a_t.rearrange("(cc p) e -> p cc e", p=128)))
            sG.dinc(g.dma_start(out=BfT[:].rearrange("p (ec o) -> p ec o", ec=2),
                                in_=self.bf_t.rearrange("(ec p) o -> p ec o", p=128)))
        blk.gpsimd(cload)
        self.const_loaded_g = sG.n

        self.emit_ln(self.xin, with_xa=True)
        if PHASES == "ALN":
            scA1.close()
            self.emit_z2_gather()
            return self._trunc(scA, es)
        self.emit_mm_stream(self.w_blk, zT, lora=True, biasmm=True, res_scale=0.5,
                            xe_src=self.xin, out_dst=self.y1d)
        ffn1_store_s = self.mm_stream_last_store
        ffn1_pe = self.mm_stream_last_pe
        scA1.close()

        self.zT_free_pe = ffn1_pe
        self.emit_ln(self.y1d, with_xa=False, src_wait_s=ffn1_store_s)
        self.zT_free_pe = 0
        self.emit_z2_gather()
        if PHASES == "A":
            return self._trunc(scA, es)
        scA.close()

        scQ = ExitStack()
        eQ = scQ.enter_context
        self.wqb = [eQ(nc.sbuf_tensor(f"wqb{i}", [128, 4096], BF16)) for i in range(2)]
        self.zgb = [eQ(nc.sbuf_tensor(f"zgb{i}", [128, 16384], BF16)) for i in range(2)]
        self.cosf = eQ(nc.sbuf_tensor("cosf", [128, T], F32))
        self.sinf = eQ(nc.sbuf_tensor("sinf", [128, T], F32))
        self.rt1 = eQ(nc.sbuf_tensor("rt1", [128, 512], F32))
        self.rt2 = eQ(nc.sbuf_tensor("rt2", [128, 512], F32))
        self.robf = [eQ(nc.sbuf_tensor(f"rob{i}", [128, 512], BF16)) for i in range(2)]
        self.vbf = eQ(nc.sbuf_tensor("vbf", [128, 512], BF16))
        self.vtm = eQ(nc.sbuf_tensor("vtm", [128, 512], BF16))
        self.prefT = eQ(nc.sbuf_tensor("prefT", [128, 320], BF16))
        self.aksb = eQ(nc.sbuf_tensor("aksb", [10, 512], BF16))

        def rload(g):
            W.wait(g, "g", sS, self.z2_export_s)
            sG.dinc(g.dma_start(out=self.cosf[:], in_=self.cos_in))
            sG.dinc(g.dma_start(out=self.sinf[:], in_=self.sin_in))
            sG.dinc(g.dma_start(out=self.prefT[:].rearrange("p (cc a) -> p cc a", cc=NCC),
                                in_=self.pref_in.rearrange("(cc p) a -> p cc a", p=128)))
        blk.gpsimd(rload)
        self.rope_tables_g = sG.n
        self.pref_loaded_g = sG.n

        self.emit_qkv_hs()
        self.emit_adapter()
        if PHASES == "Q":
            return self._trunc(scQ, es)
        scQ.close()

        scB = ExitStack()
        eB = scB.enter_context
        self.ktile = [eB(nc.sbuf_tensor(f"kt{i}", [128, 512], BF16)) for i in range(3)]
        self.vtile = [eB(nc.sbuf_tensor(f"vt{i}", [128, 512], BF16)) for i in range(3)]
        self.qtile = [eB(nc.sbuf_tensor(f"qt{i}", [128, 512], BF16)) for i in range(2)]
        self.Etile = [eB(nc.sbuf_tensor(f"E{i}", [128, 2048], BF16)) for i in range(2)]
        self.easb = eB(nc.sbuf_tensor("easb", [10, 512], BF16))
        self.ean_bf = eB(nc.sbuf_tensor("ean_bf", [10, 512], BF16))
        self.lln = eB(nc.sbuf_tensor("lln", [1, 512], F32))
        self.lml = eB(nc.sbuf_tensor("lml", [1, 512], F32))
        self.gdf1 = eB(nc.sbuf_tensor("gdf1", [1, 512], F32))
        self.ga1 = eB(nc.sbuf_tensor("ga1", [1, 512], F32))
        self.ga_bf = eB(nc.sbuf_tensor("ga_bf", [1, 512], BF16))
        self.il_bf = eB(nc.sbuf_tensor("il_bf", [1, 512], BF16))
        self.nb_sb = eB(nc.sbuf_tensor("nb_sb", [128, 512], BF16))
        self.ytile = eB(nc.sbuf_tensor("ytile", [128, 512], BF16))

        self.emit_attention_hs()
        if PHASES == "B":
            return self._trunc(scB, es)
        scB.close()

        scC = ExitStack()
        eC = scC.enter_context
        self.wpb = eC(nc.sbuf_tensor("wpb", [128, HL * C], BF16))
        self.ycb = [eC(nc.sbuf_tensor(f"ycb{i}", [128, HL * 128], BF16)) for i in range(3)]
        self.pob = [eC(nc.sbuf_tensor(f"pob{i}", [128, 512], BF16)) for i in range(2)]
        self.xe = [eC(nc.sbuf_tensor(f"xec{i}", [128, 512], F32)) for i in range(2)]
        self.xe2 = [eC(nc.sbuf_tensor(f"xe2c{i}", [128, 512], F32)) for i in range(2)]
        self.ob = [eC(nc.sbuf_tensor(f"obc{i}", [128, 512], F32)) for i in range(2)]
        self.ob2 = [eC(nc.sbuf_tensor(f"ob2c{i}", [128, 512], F32)) for i in range(2)]
        self.rsb = [eC(nc.sbuf_tensor(f"rsb{i}", [128, 1024], F32)) for i in range(2)]

        self.emit_proj_hs()
        if PHASES == "C":
            return self._trunc(scC, es)
        scC.close()

        scD = ExitStack()
        eD = scD.enter_context
        zTc = eD(nc.sbuf_tensor("zTc", [128, NCC * 1024], BF16))
        self.xbuf = eD(nc.sbuf_tensor("xbufc", [128, C], F32))
        self.zbf = eD(nc.sbuf_tensor("zbc0", [128, C], BF16))
        self.wbuf = [eD(nc.sbuf_tensor(f"wbd{i}", [128, 8192], BF16)) for i in range(3)]
        self.xawT = eD(nc.sbuf_tensor("xawTc", [128, 2048], BF16))
        self.xabf = eD(nc.sbuf_tensor("xabfc", [128, 256], BF16))
        self.xe = [eD(nc.sbuf_tensor(f"xed{i}", [128, 512], F32)) for i in range(2)]
        self.ob = [eD(nc.sbuf_tensor(f"obd{i}", [128, 512], F32)) for i in range(2)]
        ATc = eD(nc.sbuf_tensor("ATc", [128, NCC * 256], BF16))
        BfTc = eD(nc.sbuf_tensor("BfTc", [128, 2 * C], BF16))
        self.zT = zTc
        self.AT, self.BfT = ATc, BfTc
        self.wbuf_free = [0, 0, 0]
        self.sq_bank_free = [0, 0, 0, 0]
        self.xe_free = [0, 0]
        self.ob_free = [0, 0]
        self.ln_prev_zbf_pe = [0, 0]
        self.zT_free_pe = 0
        self.xbuf_free_v = 0

        def cload2(g):
            W.wait(g, "g", sS, self.y2_store_s)
            W.wait(g, "g", sV, self.y2_add_v)
            sG.dinc(g.dma_start(out=ATc[:].rearrange("p (cc e) -> p cc e", cc=NCC),
                                in_=self.a_t.rearrange("(cc p) e -> p cc e", p=128)))
            sG.dinc(g.dma_start(out=BfTc[:].rearrange("p (ec o) -> p ec o", ec=2),
                                in_=self.bf_t.rearrange("(ec p) o -> p ec o", p=128)))
        blk.gpsimd(cload2)
        self.const_loaded_g = sG.n

        self.emit_ln(self.y2d, with_xa=True, src_wait_s=self.y2_store_s)
        self.emit_mm_stream(self.w_blk, zTc, lora=True, biasmm=True, res_scale=0.5,
                            xe_src=self.y2d, out_dst=self.out, xe_wait_s=self.y2_store_s)

        if DEBUG:
            rdb = eD(nc.sbuf_tensor("rdb", [128, 1024], BF16))

            def rdump(s):
                W.wait(s, "sy", sS, self.mm_stream_last_store)
                for p in range(4):
                    for t in range(NT):
                        sS.dinc(s.dma_start(out=rdb[:], in_=self.rso_ap[p][t * 128:(t + 1) * 128, :]))
                        s.wait_ge(sS.h, sS.n)
                        sS.dinc(s.dma_start(
                            out=self.rsod[p * TOK + t * 128: p * TOK + (t + 1) * 128, :],
                            in_=rdb[:]))
                        s.wait_ge(sS.h, sS.n)
            blk.sync(rdump)

        def fin(s):
            s.wait_ge(self.sS.h, self.sS.n)
        blk.sync(fin)
        scD.close()

        self.blk = None
        es.close()
        return nc



def _host_tables():
    theta = (1.0 / (10000.0 ** (np.arange(0, HD, 2, dtype=np.float32) / np.float32(HD)))).astype(np.float32)
    pos = np.arange(T, dtype=np.float32)
    idx = np.outer(pos, theta).astype(np.float32)
    idx2 = np.concatenate([idx, idx], axis=1)
    cos = np.cos(idx2).astype(np.float32).T
    sin = np.sin(idx2).astype(np.float32).T
    ss = sin.copy()
    ss[0:64, :] *= -1.0
    return cos, ss


def _core_slices(r):
    return slice(512 * r, 512 * r + 512), slice(512 * (7 - r), 512 * (7 - r) + 512)


def _make_dmask():
    m = np.zeros((128, 2048), dtype=np.float32)
    k = np.arange(128)[:, None]
    q = np.arange(512)[None, :]
    for c in range(4):
        m[:, c * 512:(c + 1) * 512] = (q >= (c * 128 + k)).astype(np.float32)
    return m.astype(ml_dtypes.bfloat16)


_NC_CACHE = {}


def _get_nc():
    key = ("v2", DEBUG)
    if key not in _NC_CACHE:
        _NC_CACHE[key] = Builder().build()
    return _NC_CACHE[key]


def kernel(L0_attn_x, L0_attn_w_attn, L0_attn_w_proj, L0_attn_adapter_wte, L0_attn_gating,
           L0_ffn_W, L0_ffn_bias, L0_ffn_A, L0_ffn_Bf, L0_ffn_coeffs, L0_ffn_scales):
    x = np.asarray(L0_attn_x, dtype=np.float32)
    w_attn = np.asarray(L0_attn_w_attn, dtype=np.float32)
    w_proj = np.asarray(L0_attn_w_proj, dtype=np.float32)
    awte = np.asarray(L0_attn_adapter_wte, dtype=np.float32)
    gating = np.asarray(L0_attn_gating, dtype=np.float32)
    Wf = np.asarray(L0_ffn_W, dtype=np.float32)
    bias = np.asarray(L0_ffn_bias, dtype=np.float32)
    A = np.asarray(L0_ffn_A, dtype=np.float32)
    Bf = np.asarray(L0_ffn_Bf, dtype=np.float32)
    coeffs = np.asarray(L0_ffn_coeffs, dtype=np.float32)
    scales = np.asarray(L0_ffn_scales, dtype=np.float32)

    w_blk = np.ascontiguousarray(Wf.T.reshape(C, NOG, 512).transpose(1, 0, 2))
    a_t = np.ascontiguousarray(A.transpose(2, 0, 1).reshape(C, NER))
    bf_t = np.ascontiguousarray(Bf.transpose(0, 2, 1).reshape(NER, C))
    pref_t = np.ascontiguousarray(awte.T)
    wT = np.ascontiguousarray(w_attn.T)
    wpT = np.ascontiguousarray(w_proj.T)

    sel = np.zeros((16, 256), dtype=np.float32)
    for ee in range(16):
        sel[ee, ee * 16:(ee + 1) * 16] = 1.0
    cos, ss = _host_tables()
    dmask = _make_dmask()
    ident = np.eye(128, dtype=np.float32).astype(ml_dtypes.bfloat16)
    ones = np.ones((128, 128), dtype=np.float32).astype(ml_dtypes.bfloat16)

    shared = {
        "w_blk": w_blk, "a_t": a_t, "bf_t": bf_t,
        "bias_in": np.ascontiguousarray(bias.reshape(1, C)),
        "coeffs_in": np.ascontiguousarray(coeffs.reshape(16, 1)),
        "scales_in": np.ascontiguousarray(scales.reshape(16, 1)),
        "sel_in": sel, "pref_in": pref_t,
        "gating_in": np.ascontiguousarray(gating.reshape(1, 1)),
        "cos_in": cos, "sin_in": ss,
        "ident_in": ident, "ones_in": ones, "dmask_in": dmask,
    }

    in_maps = []
    for r in range(R):
        s0, s1 = _core_slices(r)
        xin = np.ascontiguousarray(np.concatenate([x[0, s0, :], x[1, s1, :]], axis=0))
        blocks = []
        for p in range(3):
            for i in range(HL):
                h = 4 * r + i
                blocks.append(wT[:, p * C + h * 128: p * C + (h + 1) * 128])
        wqkv = np.ascontiguousarray(np.stack(blocks, axis=0))
        wp_my = np.ascontiguousarray(wpT[4 * r * 128:(4 * r + HL) * 128, :])
        m = dict(shared)
        m.update({"xin": xin, "wqkv_blk": wqkv, "wp_my": wp_my})
        in_maps.append(m)

    nc = _get_nc()
    res = run_bass_kernel_spmd(nc, in_maps, core_ids=list(range(R)))

    out = np.empty((B, T, C), dtype=np.float32)
    for r in range(R):
        s0, s1 = _core_slices(r)
        o = res.results[r]["out"]
        out[0, s0, :] = o[0:512, :]
        out[1, s1, :] = o[512:1024, :]
    if DEBUG:
        kernel._debug_results = res
    return out


# version 03 (final)
# speedup vs baseline: 1.0251x; 1.0251x over previous
import math
import numpy as np
import ml_dtypes

import concourse.bass as bass
import concourse.mybir as mybir
from concourse.bass_utils import run_bass_kernel_spmd
from contextlib import ExitStack

F32 = mybir.dt.float32
BF16 = mybir.dt.bfloat16
I32 = mybir.dt.int32
AF = mybir.ActivationFunctionType
ALU = mybir.AluOpType
AX = mybir.AxisListType

B, T, C, H, HD = 2, 4096, 4096, 32, 128
R = 8
TOK = 1024
NT = TOK // 128
NCC = C // 128
NOG = C // 512
NER = 256
EPS = 1e-5
HL = 4
NOC = 3 * HL
GT = T * B
NGRP = GT // 512
SMAIN = 1.0 / math.sqrt(HD)
SADAPT = 1.0 / math.sqrt(C)

DEBUG = False
PHASES = "full"


class Sem:
    def __init__(self, es, nc, name):
        self.h = es.enter_context(nc.semaphore(name))
        self.name = name
        self.n = 0

    def inc(self, inst, k=1):
        inst.then_inc(self.h, k)
        self.n += k
        return self.n

    def dinc(self, inst):
        return self.inc(inst, 16)


class Waiter:
    def __init__(self):
        self.hw = {}

    def wait(self, eng, ename, sem, val):
        if val is None or val <= 0:
            return
        key = (ename, sem.name)
        if self.hw.get(key, -1) >= val:
            return
        eng.wait_ge(sem.h, val)
        self.hw[key] = val

    def self_sync(self, eng, ename, sem, inst):
        sem.inc(inst)
        eng.wait_ge(sem.h, sem.n)
        self.hw[(ename, sem.name)] = sem.n


def grp_cols(b, chunk):
    rb = chunk if b == 0 else 7 - chunk
    return rb * 1024 + b * 512


class Builder:
    def __init__(self):
        self.nc = bass.Bass("TRN2", target_bir_lowering=False)
        self.W = Waiter()

    def emit_ln(self, src_ap, with_xa, src_wait_s=0):
        blk, W = self.blk, self.W
        sS, sG, sP, sA, sV = self.sS, self.sG, self.sP, self.sA, self.sV
        zT, zbf, xbuf = self.zT, self.zbf, self.xbuf
        T0, PO = self.T0, self.PO
        st = self.stats

        x_rdy = [0] * NT
        z_done = [0] * NT
        tr_done = [[0] * 4 for _ in range(NT)]
        ev_done = [[0] * 4 for _ in range(NT)]
        xa_mm = [0] * NT
        xa_evac = [0] * NT
        xa_tr = [0] * NT
        self.xaw_done = [0] * NT
        prev_sq_pe = self.ln_prev_zbf_pe

        for t in range(NT):
            def ld(s, t=t):
                W.wait(s, "sy", sS, src_wait_s)
                W.wait(s, "sy", sV, z_done[t - 1] if t >= 1 else self.xbuf_free_v)
                x_rdy[t] = sS.dinc(s.dma_start(
                    out=xbuf[:], in_=src_ap[t * 128:(t + 1) * 128, :]))
            blk.sync(ld)

            def v1(v, t=t):
                W.wait(v, "v", sS, x_rdy[t])
                ins = v.reduce_sum(st["sum"][:], xbuf[:], axis=AX.X)
                W.self_sync(v, "v", sV, ins)
                sV.inc(v.tensor_tensor(st["t1"][:], st["sum"][:], st["sum"][:], op=ALU.mult))
            blk.vector(v1)
            v_t1 = sV.n

            def a1(s, t=t):
                W.wait(s, "a", sS, x_rdy[t])
                W.wait(s, "a", sP, prev_sq_pe[0])
                sA.inc(s.activation(zbf[:], xbuf[:], AF.Square, accum_out=st["ssq"][:]))
            blk.scalar(a1)
            a_ssq = sA.n

            def v2(v, t=t):
                W.wait(v, "v", sA, a_ssq)
                W.wait(v, "v", sV, v_t1)
                ins = v.tensor_scalar(st["tmp"][:], st["t1"][:], -1.0 / (C * C), EPS,
                                      op0=ALU.mult, op1=ALU.add)
                W.self_sync(v, "v", sV, ins)
                sV.inc(v.scalar_tensor_tensor(st["var"][:], st["ssq"][:], 1.0 / C, st["tmp"][:],
                                              op0=ALU.mult, op1=ALU.add))
            blk.vector(v2)
            v_var = sV.n

            def a2(s, t=t):
                W.wait(s, "a", sV, v_var)
                ins = s.activation(st["lnv"][:], st["var"][:], AF.Ln)
                W.self_sync(s, "a", sA, ins)
                sA.inc(s.activation(st["rstd"][:], st["lnv"][:], AF.Exp, scale=-0.5))
            blk.scalar(a2)
            a_rstd = sA.n

            def v3(v, t=t):
                W.wait(v, "v", sA, a_rstd)
                ins = v.scalar_tensor_tensor(st["negmr"][:], st["sum"][:], -1.0 / C, st["rstd"][:],
                                             op0=ALU.mult, op1=ALU.mult)
                W.self_sync(v, "v", sV, ins)
                z_done[t] = sV.inc(v.tensor_scalar(zbf[:], xbuf[:],
                                                   st["rstd"][:], st["negmr"][:],
                                                   op0=ALU.mult, op1=ALU.add))
            blk.vector(v3)

            for bt in range(4):
                def tr(te, t=t, bt=bt):
                    W.wait(te, "t", sV, z_done[t])
                    if bt == 0:
                        prev_ev = ev_done[t - 1][3] if t >= 1 else self.t0_free_v
                        W.wait(te, "t", sA, self.xaw_done[t - 1] if (with_xa and t >= 1)
                               else self.t0_free_a)
                    else:
                        prev_ev = ev_done[t][bt - 1]
                    W.wait(te, "t", sV, prev_ev)
                    for k in range(8):
                        cc = bt * 8 + k
                        ins = te.transpose(T0[:, k * 128:(k + 1) * 128],
                                           zbf[:, cc * 128:(cc + 1) * 128], self.ident[:])
                    tr_done[t][bt] = sP.inc(ins)
                blk.tensor(tr)

                def ev(v, t=t, bt=bt):
                    W.wait(v, "v", sP, tr_done[t][bt])
                    W.wait(v, "v", sP, self.zT_free_pe)
                    for k in range(8):
                        cc = bt * 8 + k
                        ins = v.tensor_copy(zT[:, cc * 1024 + t * 128: cc * 1024 + (t + 1) * 128],
                                            T0[:, k * 128:(k + 1) * 128])
                    ev_done[t][bt] = sV.inc(ins)
                blk.vector(ev)
            prev_sq_pe[0] = tr_done[t][3]

            if with_xa:
                def xam(te, t=t):
                    W.wait(te, "t", sV, ev_done[t][3])
                    W.wait(te, "t", sG, self.const_loaded_g)
                    W.wait(te, "t", sA, xa_evac[t - 1] if t >= 1 else 0)
                    for cc in range(NCC):
                        ins = te.matmul(PO[:, 0:256],
                                        zT[:, cc * 1024 + t * 128: cc * 1024 + (t + 1) * 128],
                                        self.AT[:, cc * 256:(cc + 1) * 256],
                                        start=(cc == 0), stop=(cc == NCC - 1))
                    xa_mm[t] = sP.inc(ins)
                blk.tensor(xam)

                def xae(s, t=t):
                    W.wait(s, "a", sP, xa_mm[t])
                    xa_evac[t] = sA.inc(s.activation(self.xabf[:], PO[:, 0:256], AF.Copy))
                blk.scalar(xae)

                def xat(te, t=t):
                    W.wait(te, "t", sA, xa_evac[t])
                    W.wait(te, "t", sA, self.xaw_done[t - 1] if t >= 1 else 0)
                    W.wait(te, "t", sV, ev_done[t][3])
                    for j in range(2):
                        ins = te.transpose(T0[:, j * 128:(j + 1) * 128],
                                           self.xabf[:, j * 128:(j + 1) * 128], self.ident[:])
                    xa_tr[t] = sP.inc(ins)
                blk.tensor(xat)

                def xaw(s, t=t):
                    W.wait(s, "a", sP, xa_tr[t])
                    W.wait(s, "a", sV, self.wer_ready_v)
                    for j in range(2):
                        ins = s.activation(self.xawT[:, j * 1024 + t * 128: j * 1024 + (t + 1) * 128],
                                           T0[:, j * 128:(j + 1) * 128], AF.Copy,
                                           scale=self.w_er[:, j:j + 1])
                    self.xaw_done[t] = sA.inc(ins)
                blk.scalar(xaw)

        self.xbuf_free_v = z_done[NT - 1]
        self.ln_last_z_evac = ev_done[NT - 1][3]
        self.ln_last_xaw = self.xaw_done[NT - 1] if with_xa else 0
        self.t0_free_v = ev_done[NT - 1][3]
        self.t0_free_a = self.xaw_done[NT - 1] if with_xa else 0

    def emit_mm_stream(self, w_dram, lhs_sb, lora, biasmm, res_scale,
                       xe_src, out_dst, xe_wait_s=0, extra_pe_wait=None):
        blk, W = self.blk, self.W
        sS, sG, sP, sA, sV, sX = self.sS, self.sG, self.sP, self.sA, self.sV, self.sX
        SQ = self.SQ
        wbuf = self.wbuf

        lhs_v = self.ln_last_z_evac
        xaw_a = self.ln_last_xaw if lora else 0
        w_rdy = {}
        mm_done = {}
        wbuf_free = self.wbuf_free
        bank_free = self.sq_bank_free
        xe_rdy = {}
        xe_free = self.xe_free
        ob_free = self.ob_free

        def wload(g, og, hf):
            bi = (og * 2 + hf) % 3
            W.wait(g, "g", sP, wbuf_free[bi])
            src = w_dram[og, hf * 2048:(hf + 1) * 2048, :].rearrange("(cc p) o -> p cc o", p=128)
            dst = wbuf[bi][:].rearrange("p (cc o) -> p cc o", cc=16)
            w_rdy[(og, hf)] = sG.dinc(g.dma_start(out=dst, in_=src))

        blk.gpsimd(lambda g: [wload(g, 0, 0), wload(g, 0, 1)])

        for og in range(NOG):
            if og + 1 < NOG:
                blk.gpsimd(lambda g, og=og: wload(g, og + 1, 0))

            for t in range(NT):
                bank = ((t // 2) % 2) * 2 + (t % 2)
                sl = t % 2

                def xl(s, og=og, t=t, sl=sl):
                    W.wait(s, "x", sS, xe_wait_s)
                    W.wait(s, "x", sV, xe_free[sl])
                    xe_rdy[(og, t)] = sX.dinc(s.dma_start(
                        out=self.xe[sl][:],
                        in_=xe_src[t * 128:(t + 1) * 128, og * 512:(og + 1) * 512]))
                blk.scalar(xl)

                def mm(te, og=og, t=t, bank=bank):
                    W.wait(te, "t", sG, w_rdy[(og, 0)])
                    W.wait(te, "t", sV, lhs_v)
                    if lora:
                        W.wait(te, "t", sA, xaw_a)
                    if extra_pe_wait is not None:
                        extra_pe_wait(te)
                    W.wait(te, "t", sV, bank_free[bank])
                    n_mm = 32 + (2 if lora else 0) + (1 if biasmm else 0)
                    k = 0
                    for hf in range(2):
                        bi = (og * 2 + hf) % 3
                        if hf == 1:
                            W.wait(te, "t", sG, w_rdy[(og, 1)])
                        for cc16 in range(16):
                            cc = hf * 16 + cc16
                            k += 1
                            ins = te.matmul(
                                SQ[:, bank * 512:(bank + 1) * 512],
                                lhs_sb[:, cc * 1024 + t * 128: cc * 1024 + (t + 1) * 128],
                                wbuf[bi][:, cc16 * 512:(cc16 + 1) * 512],
                                start=(k == 1), stop=(k == n_mm))
                    if lora:
                        for j in range(2):
                            k += 1
                            ins = te.matmul(
                                SQ[:, bank * 512:(bank + 1) * 512],
                                self.xawT[:, j * 1024 + t * 128: j * 1024 + (t + 1) * 128],
                                self.BfT[:, j * 4096 + og * 512: j * 4096 + (og + 1) * 512],
                                start=False, stop=(k == n_mm))
                    if biasmm:
                        k += 1
                        ins = te.matmul(SQ[:, bank * 512:(bank + 1) * 512],
                                        self.onesb[0:1, 0:128],
                                        self.bias_bf[0:1, og * 512:(og + 1) * 512],
                                        start=False, stop=True)
                    mm_done[(og, t)] = sP.inc(ins)
                blk.tensor(mm)

                def epi(v, og=og, t=t, bank=bank, sl=sl):
                    W.wait(v, "v", sP, mm_done[(og, t)])
                    W.wait(v, "v", sX, xe_rdy[(og, t)])
                    W.wait(v, "v", sS, ob_free[sl])
                    if res_scale == 1.0:
                        ins = v.tensor_add(self.ob[sl][:], self.xe[sl][:],
                                           SQ[:, bank * 512:(bank + 1) * 512])
                    else:
                        ins = v.scalar_tensor_tensor(self.ob[sl][:],
                                                     SQ[:, bank * 512:(bank + 1) * 512],
                                                     res_scale, self.xe[sl][:],
                                                     op0=ALU.mult, op1=ALU.add)
                    val = sV.inc(ins)
                    bank_free[bank] = val
                    xe_free[sl] = val
                blk.vector(epi)
                epi_v = sV.n

                def ost(s, og=og, t=t, sl=sl, epi_v=epi_v):
                    W.wait(s, "sy", sV, epi_v)
                    ob_free[sl] = sS.dinc(s.dma_start(
                        out=out_dst[t * 128:(t + 1) * 128, og * 512:(og + 1) * 512],
                        in_=self.ob[sl][:]))
                blk.sync(ost)

            wbuf_free[(og * 2) % 3] = mm_done[(og, NT - 1)]
            wbuf_free[(og * 2 + 1) % 3] = mm_done[(og, NT - 1)]
            if og + 1 < NOG:
                blk.gpsimd(lambda g, og=og: wload(g, og + 1, 1))

        self.mm_stream_last_pe = mm_done[(NOG - 1, NT - 1)]
        self.mm_stream_last_store = sS.n

    def emit_z2_gather(self):
        blk, W = self.blk, self.W
        sS, sCC, sV = self.sS, self.sCC, self.sV

        def zst(s):
            W.wait(s, "sy", sV, self.ln_last_z_evac)
            for cc in range(NCC):
                ins = s.dma_start(out=self.z2d_ap[cc * 128:(cc + 1) * 128, :],
                                  in_=self.zT[:, cc * 1024:(cc + 1) * 1024])
                sS.dinc(ins)
        blk.sync(zst)
        z_st = sS.n
        self.z2_export_s = z_st

        def cc(g):
            W.wait(g, "g", sS, z_st)
            sCC.inc(g.collective_compute("AllGather", ALU.bypass,
                                          ins=[self.z2d_ap.opt()], outs=[self.z2g_ap.opt()],
                                          replica_groups=[list(range(R))]))
        blk.gpsimd(cc)
        self.cc_z_done = sCC.n

    def emit_qkv_hs(self):
        blk, W = self.blk, self.W
        sS, sG, sP, sA, sV = self.sS, self.sG, self.sP, self.sA, self.sV
        SQ, T0 = self.SQ, self.T0

        w_rdy = {}
        wq_free = [0, 0]
        zg_rdy = {}
        zg_free = [0, 0]
        mm_done = {}
        bank_free_v = [0] * 4
        bank_free_a = [0] * 4
        robf_free = [0, 0]
        vst = {"tr": 0, "ev": 0, "st": 0, "vbf": 0}

        def wload(g, grp, oc):
            sl = (grp * NOC + oc) % 2
            W.wait(g, "g", sP, wq_free[sl])
            src = self.wqkv_blk[oc].rearrange("(cc p) o -> p cc o", p=128)
            dst = self.wqb[sl][:].rearrange("p (cc o) -> p cc o", cc=32)
            w_rdy[(grp, oc)] = sG.dinc(g.dma_start(out=dst, in_=src))

        def zload(g, grp):
            sl = grp % 2
            W.wait(g, "g", sP, zg_free[sl])
            W.wait(g, "g", self.sCC, self.cc_z_done)
            rb, tl = grp // 2, grp % 2
            src = self.z2g_ap[rb * 4096:(rb + 1) * 4096,
                              tl * 512:(tl + 1) * 512].rearrange("(cc p) t -> p cc t", p=128)
            dst = self.zgb[sl][:].rearrange("p (cc t) -> p cc t", cc=32)
            zg_rdy[grp] = sG.dinc(g.dma_start(out=dst, in_=src))

        blk.gpsimd(lambda g: [zload(g, 0), wload(g, 0, 0), wload(g, 0, 1)])

        for grp in range(NGRP):
            rb, tl = grp // 2, grp % 2
            chunk = rb if tl == 0 else 7 - rb
            poscol = chunk * 512
            dstcols = rb * 1024 + tl * 512
            if grp + 1 < NGRP:
                blk.gpsimd(lambda g, grp=grp: zload(g, grp + 1))

            for oc in range(NOC):
                zsl = grp % 2
                wsl = (grp * NOC + oc) % 2
                bank = (grp * NOC + oc) % 4
                ng, no = (grp, oc + 1) if oc + 1 < NOC else (grp + 1, 0)
                if ng < NGRP:
                    blk.gpsimd(lambda g, ng=ng, no=no: wload(g, ng, no))

                def mm(te, grp=grp, oc=oc, zsl=zsl, wsl=wsl, bank=bank):
                    W.wait(te, "t", sG, w_rdy[(grp, oc)])
                    W.wait(te, "t", sG, zg_rdy[grp])
                    W.wait(te, "t", sV, bank_free_v[bank])
                    W.wait(te, "t", sA, bank_free_a[bank])
                    for cc in range(NCC):
                        ins = te.matmul(SQ[:, bank * 512:(bank + 1) * 512],
                                        self.wqb[wsl][:, cc * 128:(cc + 1) * 128],
                                        self.zgb[zsl][:, cc * 512:(cc + 1) * 512],
                                        start=(cc == 0), stop=(cc == NCC - 1))
                    mm_done[(grp, oc)] = sP.inc(ins)
                    wq_free[wsl] = mm_done[(grp, oc)]
                blk.tensor(mm)

                hl = oc % HL
                if oc < 8:
                    sl = (grp * NOC + oc) % 2

                    def rope(v, grp=grp, oc=oc, bank=bank, sl=sl, poscol=poscol):
                        W.wait(v, "v", sP, mm_done[(grp, oc)])
                        W.wait(v, "v", sS, robf_free[sl])
                        W.wait(v, "v", sG, self.rope_tables_g)
                        bk = SQ[:, bank * 512:(bank + 1) * 512]
                        v.tensor_tensor(self.rt1[:], bk, self.cosf[:, poscol:poscol + 512],
                                        op=ALU.mult)
                        v.tensor_tensor(self.rt2[0:64, :], bk[64:128, :],
                                        self.sinf[0:64, poscol:poscol + 512], op=ALU.mult)
                        v.tensor_tensor(self.rt2[64:128, :], bk[0:64, :],
                                        self.sinf[64:128, poscol:poscol + 512], op=ALU.mult)
                        ins = v.tensor_add(self.robf[sl][:], self.rt1[:], self.rt2[:])
                        bank_free_v[bank] = sV.inc(ins)
                    blk.vector(rope)
                    rope_v = sV.n

                    def rst(s, oc=oc, hl=hl, sl=sl, rope_v=rope_v, dstcols=dstcols):
                        W.wait(s, "sy", sV, rope_v)
                        dst = self.qdl_ap if oc < HL else self.kd_ap
                        robf_free[sl] = sS.dinc(s.dma_start(
                            out=dst[hl * 128:(hl + 1) * 128, dstcols:dstcols + 512],
                            in_=self.robf[sl][:]))
                    blk.sync(rst)
                else:
                    def vc(s, grp=grp, oc=oc, bank=bank):
                        W.wait(s, "a", sP, mm_done[(grp, oc)])
                        W.wait(s, "a", sP, vst["tr"])
                        bank_free_a[bank] = sA.inc(
                            s.activation(self.vbf[:], SQ[:, bank * 512:(bank + 1) * 512],
                                         AF.Copy))
                        vst["vbf"] = bank_free_a[bank]
                    blk.scalar(vc)

                    def vt(te):
                        W.wait(te, "t", sA, vst["vbf"])
                        W.wait(te, "t", sV, vst["ev"] if vst["ev"] else self.t0_free_v)
                        W.wait(te, "t", sA, self.t0_free_a)
                        for j in range(4):
                            ins = te.transpose(T0[:, j * 128:(j + 1) * 128],
                                               self.vbf[:, j * 128:(j + 1) * 128], self.ident[:])
                        vst["tr"] = sP.inc(ins)
                    blk.tensor(vt)

                    def vev(v):
                        W.wait(v, "v", sP, vst["tr"])
                        W.wait(v, "v", sS, vst["st"])
                        for j in range(4):
                            ins = v.tensor_copy(self.vtm[:, j * 128:(j + 1) * 128],
                                                T0[:, j * 128:(j + 1) * 128])
                        vst["ev"] = sV.inc(ins)
                    blk.vector(vev)

                    def vsto(s, hl=hl, dstcols=dstcols):
                        W.wait(s, "sy", sV, vst["ev"])
                        for j in range(4):
                            ins = s.dma_start(
                                out=self.vd_ap[dstcols + j * 128: dstcols + (j + 1) * 128,
                                               hl * 128:(hl + 1) * 128],
                                in_=self.vtm[:, j * 128:(j + 1) * 128])
                            vst["st"] = sS.dinc(ins)
                    blk.sync(vsto)

            zg_free[grp % 2] = mm_done[(grp, NOC - 1)]

        self.qkv_last_pe = mm_done[(NGRP - 1, NOC - 1)]
        self.t0_free_v = vst["ev"]

    def emit_adapter(self):
        blk, W = self.blk, self.W
        sS, sG, sP, sA, sV = self.sS, self.sG, self.sP, self.sA, self.sV
        PA, T0 = self.PA, self.T0
        ad = {"aw": 0, "mm": 0, "ev": 0, "tr": 0}

        for part in range(2):
            def awl(g, part=part):
                W.wait(g, "g", sP, self.qkv_last_pe if part == 0 else ad["mm"])
                for i in range(4):
                    oc = 4 + part * 4 + i
                    src = self.wqkv_blk[oc].rearrange("(cc p) o -> p cc o", p=128)
                    dst = self.zgb[0][:].rearrange("p (cc q) -> p cc q", cc=32)[
                        :, :, i * 128:(i + 1) * 128]
                    ad["aw"] = sG.dinc(g.dma_start(out=dst, in_=src))
            blk.gpsimd(awl)

            def amm(te, part=part):
                W.wait(te, "t", sG, ad["aw"])
                W.wait(te, "t", sG, self.pref_loaded_g)
                if part == 1:
                    W.wait(te, "t", sA, ad["ev"])
                    W.wait(te, "t", sP, ad["tr"])
                for cc in range(NCC):
                    ins = te.matmul(PA[0:10, :], self.prefT[:, cc * 10:(cc + 1) * 10],
                                    self.zgb[0][:, cc * 512:(cc + 1) * 512],
                                    start=(cc == 0), stop=(cc == NCC - 1))
                ad["mm"] = sP.inc(ins)
            blk.tensor(amm)

            def aev(s, part=part):
                W.wait(s, "a", sP, ad["mm"])
                ad["ev"] = sA.inc(s.activation(self.aksb[:], PA[0:10, :], AF.Copy))
            blk.scalar(aev)

            if part == 0:
                def atr(te):
                    W.wait(te, "t", sA, ad["ev"])
                    W.wait(te, "t", sV, self.t0_free_v)
                    W.wait(te, "t", sA, self.t0_free_a)
                    for j in range(4):
                        ins = te.transpose(T0[:, j * 128: j * 128 + 10],
                                           self.aksb[0:10, j * 128:(j + 1) * 128],
                                           self.ident[0:10, 0:10])
                    ad["tr"] = sP.inc(ins)
                blk.tensor(atr)

                def aevt(v):
                    W.wait(v, "v", sP, ad["tr"])
                    for j in range(4):
                        ins = v.tensor_copy(self.akt[:, j * 10:(j + 1) * 10],
                                            T0[:, j * 128: j * 128 + 10])
                    self.ak_ready_v = sV.inc(ins)
                    self.t0_free_v = self.ak_ready_v
                blk.vector(aevt)
            else:
                def avc(v):
                    W.wait(v, "v", sA, ad["ev"])
                    self.av_ready_v = sV.inc(v.tensor_copy(self.avt[:], self.aksb[:]))
                blk.vector(avc)
        self.adapter_pe = ad["mm"]
        self.qkv_spill_s = self.sS.n

    def emit_attention_hs(self):
        blk, W = self.blk, self.W
        sS, sG, sP, sA, sV = self.sS, self.sG, self.sP, self.sA, self.sV
        SQ, PO, PL, PA = self.SQ, self.PO, self.PL, self.PA
        kt, vt, qt = self.ktile, self.vtile, self.qtile
        E = self.Etile

        units = [(b, hl, qi) for b in range(B) for hl in range(HL) for qi in range(8)]
        tasks = []
        for ui, (b, hl, qi) in enumerate(units):
            for kb in range(qi + 1):
                tasks.append((ui, kb))
        tasknum = {t: i for i, t in enumerate(tasks)}

        ld_rdy = {}
        q_rdy = {}
        qk_done = {}
        exp_done = {}
        msk_done = {}
        pv_done = {}
        task_prev_pe = {}
        unit_q_free = {}
        u = {"nrm_v": 0, "nbe_a": 0, "nil_a": 0, "ysto": 0, "exp_prev": 0,
             "sa": 0, "ea": 0, "la": 0, "ga": 0, "gab": 0, "bc": 0, "ean": 0,
             "apv": 0, "nb": 0}

        def tload(s, ui, kb):
            b, hl, qi = units[ui]
            sl = tasknum[(ui, kb)] % 3
            W.wait(s, "sy", sP, task_prev_pe.get((ui, kb), 0))
            cols = grp_cols(b, kb)
            sS.dinc(s.dma_start(out=kt[sl][:],
                                in_=self.kd_ap[hl * 128:(hl + 1) * 128, cols:cols + 512]))
            ld = s.dma_start(out=vt[sl][:].rearrange("p (c d) -> p c d", c=4),
                             in_=self.vd_ap[cols:cols + 512,
                                            hl * 128:(hl + 1) * 128].rearrange(
                                 "(c p) d -> p c d", p=128))
            ld_rdy[(ui, kb)] = sS.dinc(ld)

        def qload(s, ui):
            b, hl, qi = units[ui]
            W.wait(s, "sy", sP, unit_q_free.get(ui, 0))
            cols = grp_cols(b, qi)
            q_rdy[ui] = sS.dinc(s.dma_start(
                out=qt[ui % 2][:],
                in_=self.qdl_ap[hl * 128:(hl + 1) * 128, cols:cols + 512]))

        def first(s):
            W.wait(s, "sy", sP, self.adapter_pe)
            W.wait(s, "sy", sV, self.av_ready_v)
            qload(s, 0)
            tload(s, 0, 0)
        blk.sync(first)

        for ui, (b, hl, qi) in enumerate(units):
            if ui + 1 < len(units):
                blk.sync(lambda s, ui=ui: qload(s, ui + 1))

            for kb in range(qi + 1):
                tn = tasknum[(ui, kb)]
                sl = tn % 3
                esl = tn % 2
                if tn + 1 < len(tasks):
                    nui, nkb = tasks[tn + 1]
                    if tn - 2 >= 0:
                        task_prev_pe[tasks[tn + 1]] = pv_done.get(tasks[tn - 2], 0)
                    blk.sync(lambda s, nui=nui, nkb=nkb: tload(s, nui, nkb))

                def qk(te, ui=ui, kb=kb, sl=sl):
                    W.wait(te, "t", sS, ld_rdy[(ui, kb)])
                    W.wait(te, "t", sS, q_rdy[ui])
                    W.wait(te, "t", sA, u["exp_prev"])
                    for c in range(4):
                        ins = te.matmul(SQ[:, c * 512:(c + 1) * 512],
                                        kt[sl][:, c * 128:(c + 1) * 128], qt[ui % 2][:],
                                        start=True, stop=True)
                    qk_done[(ui, kb)] = sP.inc(ins)
                blk.tensor(qk)

                def ex(s, ui=ui, kb=kb, esl=esl, tn=tn):
                    W.wait(s, "a", sP, qk_done[(ui, kb)])
                    if tn >= 2:
                        W.wait(s, "a", sP, pv_done.get(tasks[tn - 2], 0))
                    exp_done[(ui, kb)] = sA.inc(
                        s.activation(E[esl][:], SQ[:, 0:2048], AF.Exp, scale=SMAIN))
                    u["exp_prev"] = exp_done[(ui, kb)]
                blk.scalar(ex)

                if kb == qi:
                    def mk(v, ui=ui, kb=kb, esl=esl):
                        W.wait(v, "v", sA, exp_done[(ui, kb)])
                        msk_done[(ui, kb)] = sV.inc(
                            v.tensor_tensor(E[esl][:], E[esl][:], self.dmask[:], op=ALU.mult))
                    blk.vector(mk)

                def pv(te, ui=ui, kb=kb, sl=sl, esl=esl, qi=qi):
                    W.wait(te, "t", sA, exp_done[(ui, kb)])
                    if kb == qi:
                        W.wait(te, "t", sV, msk_done[(ui, kb)])
                    if kb == 0:
                        W.wait(te, "t", sV, u["nrm_v"])
                        W.wait(te, "t", sA, u["nil_a"])
                    for c in range(4):
                        te.matmul(PO[:, :], vt[sl][:, c * 128:(c + 1) * 128],
                                  E[esl][:, c * 512:(c + 1) * 512],
                                  start=(kb == 0 and c == 0), stop=False)
                    for c in range(4):
                        ins = te.matmul(PL[0:1, :], self.onesb[:, 0:1],
                                        E[esl][:, c * 512:(c + 1) * 512],
                                        start=(kb == 0 and c == 0),
                                        stop=(kb == qi and c == 3))
                    pv_done[(ui, kb)] = sP.inc(ins)
                blk.tensor(pv)

            def asa(te, ui=ui, hl=hl):
                W.wait(te, "t", sV, self.ak_ready_v)
                W.wait(te, "t", sV, self.av_ready_v)
                W.wait(te, "t", sA, u["nbe_a"])
                ins = te.matmul(PA[0:10, :], self.akt[:, hl * 10:(hl + 1) * 10],
                                qt[ui % 2][:], start=True, stop=True)
                u["sa"] = sP.inc(ins)
            blk.tensor(asa)

            def aex(s, ui=ui):
                W.wait(s, "a", sP, u["sa"])
                u["ea"] = sA.inc(s.activation(self.easb[:], PA[0:10, :], AF.Exp,
                                              scale=SADAPT))
            blk.scalar(aex)

            def ala(te, ui=ui):
                W.wait(te, "t", sA, u["ea"])
                u["la"] = sP.inc(te.matmul(PA[32:33, :], self.onesb[0:10, 0:1],
                                           self.easb[:], start=True, stop=True))
            blk.tensor(ala)

            def nil(s, ui=ui, qi=qi):
                W.wait(s, "a", sP, pv_done[(ui, qi)])
                ins = s.activation(self.lml[:], PL[0:1, :], AF.Ln)
                W.self_sync(s, "a", sA, ins)
                u["nil_a"] = sA.inc(s.activation(self.il_bf[:], self.lml[:],
                                                 AF.Exp, scale=-1.0))
            blk.scalar(nil)

            def aln(s, ui=ui):
                W.wait(s, "a", sP, u["la"])
                u["lna"] = sA.inc(s.activation(self.lln[:], PA[32:33, :], AF.Ln))
            blk.scalar(aln)

            def gdf(v, ui=ui):
                W.wait(v, "v", sA, u["lna"])
                u["gdv"] = sV.inc(v.tensor_tensor(self.gdf1[:], self.lml[:], self.lln[:],
                                                  op=ALU.subtract))
            blk.vector(gdf)

            def gex(s, ui=ui):
                W.wait(s, "a", sV, u["gdv"])
                u["ga"] = sA.inc(s.activation(self.ga1[:], self.gdf1[:], AF.Exp))
            blk.scalar(gex)

            def agv(v, ui=ui):
                W.wait(v, "v", sA, u["ga"])
                u["gab"] = sV.inc(v.tensor_single_scalar(
                    self.ga_bf[:], self.ga1[:], self.gating_sb[0:1, 0:1], op=ALU.mult))
            blk.vector(agv)

            def abc(te, ui=ui):
                W.wait(te, "t", sV, u["gab"])
                u["bc"] = sP.inc(te.matmul(PA[0:10, :], self.onesb[0:1, 0:10],
                                           self.ga_bf[:], start=True, stop=True))
            blk.tensor(abc)

            def aen(v, ui=ui):
                W.wait(v, "v", sP, u["bc"])
                u["ean"] = sV.inc(v.tensor_tensor(self.ean_bf[:], self.easb[:],
                                                  PA[0:10, :], op=ALU.mult))
            blk.vector(aen)

            def apv(te, ui=ui, hl=hl):
                W.wait(te, "t", sV, u["ean"])
                ins = te.matmul(PO[:, :], self.avt[0:10, hl * 128:(hl + 1) * 128],
                                self.ean_bf[:], start=False, stop=True)
                u["apv"] = sP.inc(ins)
            blk.tensor(apv)
            unit_q_free[ui + 2] = u["apv"]

            def nbm(te, ui=ui):
                W.wait(te, "t", sA, u["nil_a"])
                u["nb"] = sP.inc(te.matmul(PA[:, :], self.onesb[0:1, 0:128],
                                           self.il_bf[:], start=True, stop=True))
            blk.tensor(nbm)

            def nbe(s, ui=ui):
                W.wait(s, "a", sP, u["nb"])
                W.wait(s, "a", sV, u["nrm_v"])
                u["nbe_a"] = sA.inc(s.activation(self.nb_sb[:], PA[:, :], AF.Copy))
            blk.scalar(nbe)

            def nmu(v, ui=ui):
                W.wait(v, "v", sA, u["nbe_a"])
                W.wait(v, "v", sP, u["apv"])
                W.wait(v, "v", sS, u["ysto"])
                u["nrm_v"] = sV.inc(v.tensor_tensor(self.ytile[:], PO[:, :],
                                                    self.nb_sb[:], op=ALU.mult))
            blk.vector(nmu)

            def yst(s, ui=ui, b=b, hl=hl, qi=qi):
                W.wait(s, "sy", sV, u["nrm_v"])
                cols = grp_cols(b, qi)
                u["ysto"] = sS.dinc(s.dma_start(
                    out=self.yl_ap[hl * 128:(hl + 1) * 128, cols:cols + 512],
                    in_=self.ytile[:]))
            blk.sync(yst)

        self.attn_last_s = sS.n
        self.attn_last_v = sV.n

    def emit_proj_hs(self):
        blk, W = self.blk, self.W
        sS, sG, sP, sA, sV = self.sS, self.sG, self.sP, self.sA, self.sV
        SQ = self.SQ

        y_rdy = {}
        mm_done = {}
        ev_done = {}
        st_done = {}
        bank_free = [0] * 4
        ob_free = [0, 0]
        ycb_free = [0, 0, 0]

        def wpl(g):
            W.wait(g, "g", sS, self.attn_last_s)
            W.wait(g, "g", sV, self.attn_last_v)
            src = self.wp_my.rearrange("(hl p) o -> p hl o", p=128)
            dst = self.wpb[:].rearrange("p (hl o) -> p hl o", hl=HL)
            sG.dinc(g.dma_start(out=dst, in_=src))
        blk.gpsimd(wpl)
        wp_rdy = self.sG.n

        def yld(s, t):
            sl = t % 3
            W.wait(s, "sy", sS, self.attn_last_s)
            W.wait(s, "sy", sP, ycb_free[sl])
            src = self.yl_ap[:, t * 128:(t + 1) * 128].rearrange("(hl p) t -> p hl t", p=128)
            dst = self.ycb[sl][:].rearrange("p (hl t) -> p hl t", hl=HL)
            y_rdy[t] = sS.dinc(s.dma_start(out=dst, in_=src))

        blk.sync(lambda s: (yld(s, 0), yld(s, 1)))

        NTG = GT // 128
        for t in range(NTG):
            sl = t % 3
            if t + 2 < NTG:
                blk.sync(lambda s, t=t: yld(s, t + 2))

            for og in range(NOG):
                bank = (t * NOG + og) % 4
                osl = (t * NOG + og) % 2

                def mm(te, t=t, og=og, sl=sl, bank=bank):
                    W.wait(te, "t", sS, y_rdy[t])
                    W.wait(te, "t", sG, wp_rdy)
                    W.wait(te, "t", sV, bank_free[bank])
                    for hl in range(HL):
                        ins = te.matmul(
                            SQ[:, bank * 512:(bank + 1) * 512],
                            self.ycb[sl][:, hl * 128:(hl + 1) * 128],
                            self.wpb[:, hl * 4096 + og * 512: hl * 4096 + (og + 1) * 512],
                            start=(hl == 0), stop=(hl == HL - 1))
                    mm_done[(t, og)] = sP.inc(ins)
                    if og == NOG - 1:
                        ycb_free[sl] = mm_done[(t, og)]
                blk.tensor(mm)

                def ev(v, t=t, og=og, bank=bank, osl=osl):
                    W.wait(v, "v", sP, mm_done[(t, og)])
                    W.wait(v, "v", sS, ob_free[osl])
                    ins = v.tensor_copy(self.pob[osl][:],
                                        SQ[:, bank * 512:(bank + 1) * 512])
                    ev_done[(t, og)] = sV.inc(ins)
                    bank_free[bank] = ev_done[(t, og)]
                blk.vector(ev)

                def pst(s, t=t, og=og, osl=osl):
                    W.wait(s, "sy", sV, ev_done[(t, og)])
                    ob_free[osl] = sS.dinc(s.dma_start(
                        out=self.prtl_ap[og // 2][t * 128:(t + 1) * 128,
                                                  (og % 2) * 512:(og % 2) * 512 + 512],
                        in_=self.pob[osl][:]))
                    st_done[(t, og)] = sS.n
                blk.sync(pst)

        self.rs_done = [0] * 4

        def rs(g):
            for p in range(4):
                W.wait(g, "g", sS, st_done[(NTG - 1, 2 * p + 1)])
                ins = g.collective_compute(
                    "ReduceScatter", ALU.add,
                    ins=[self.prtl_ap[p].opt()],
                    outs=[self.rso_ap[p].opt()],
                    replica_groups=[list(range(R))])
                self.rs_done[p] = self.sCC.inc(ins)
        blk.gpsimd(rs)

        rb_free = [0, 0]
        xb_free = [0, 0]
        self._y2st = {}
        for p in range(4):
            for t in range(NT):
                sl = (p * NT + t) % 2

                def l1(s, p=p, t=t, sl=sl):
                    W.wait(s, "sy", self.sCC, self.rs_done[p])
                    W.wait(s, "sy", sV, xb_free[sl])
                    sS.dinc(s.dma_start(
                        out=self.xe[sl][:],
                        in_=self.y1d[t * 128:(t + 1) * 128, p * 1024:p * 1024 + 512]))
                    self._y2l1 = sS.dinc(s.dma_start(
                        out=self.xe2[sl][:],
                        in_=self.y1d[t * 128:(t + 1) * 128, p * 1024 + 512:(p + 1) * 1024]))
                blk.sync(l1)

                def l2(g, p=p, t=t, sl=sl):
                    W.wait(g, "g", self.sCC, self.rs_done[p])
                    W.wait(g, "g", sV, rb_free[sl])
                    self._y2l2 = sG.dinc(g.dma_start(
                        out=self.rsb[sl][:],
                        in_=self.rso_ap[p][t * 128:(t + 1) * 128, :]))
                blk.gpsimd(l2)

                def add(v, p=p, t=t, sl=sl):
                    W.wait(v, "v", sS, self._y2l1)
                    W.wait(v, "v", sG, self._y2l2)
                    W.wait(v, "v", sS, self._y2st.get((p * NT + t - 2), 0))
                    v.tensor_add(self.ob[sl][:], self.xe[sl][:], self.rsb[sl][:, 0:512])
                    ins = v.tensor_add(self.ob2[sl][:], self.xe2[sl][:],
                                       self.rsb[sl][:, 512:1024])
                    val = sV.inc(ins)
                    xb_free[sl] = val
                    rb_free[sl] = val
                    self._y2add = val
                blk.vector(add)

                def sto(s, p=p, t=t, sl=sl):
                    W.wait(s, "sy", sV, self._y2add)
                    sS.dinc(s.dma_start(
                        out=self.y2d[t * 128:(t + 1) * 128, p * 1024:p * 1024 + 512],
                        in_=self.ob[sl][:]))
                    self._y2st[p * NT + t] = sS.dinc(s.dma_start(
                        out=self.y2d[t * 128:(t + 1) * 128, p * 1024 + 512:(p + 1) * 1024],
                        in_=self.ob2[sl][:]))
                blk.sync(sto)

        self.y2_store_s = sS.n
        self.y2_add_v = sV.n

    def _trunc(self, es, outer_es):
        nc, blk = self.nc, self.blk
        tz = es.enter_context(nc.sbuf_tensor("tzfin", [128, 512], F32))

        def cp(g):
            g.memset(tz[:], 0.0)
            for t in range(NT):
                for cq in range(C // 512):
                    self.sG.dinc(g.dma_start(
                        out=self.out[t * 128:(t + 1) * 128, cq * 512:(cq + 1) * 512],
                        in_=tz[:]))
        blk.gpsimd(cp)

        def fin(s):
            for sem in [self.sS, self.sG, self.sP, self.sA, self.sV, self.sX, self.sCC]:
                if sem.n > 0:
                    s.wait_ge(sem.h, sem.n)
        blk.sync(fin)
        es.close()
        self.blk = None
        outer_es.close()
        return nc

    def build(self):
        nc = self.nc
        es = ExitStack()

        def din(name, shape, dt):
            return nc.dram_tensor(name, shape, dt, kind="ExternalInput").ap()

        self.xin = din("xin", [TOK, C], F32)
        self.wqkv_blk = din("wqkv_blk", [NOC, C, 128], F32)
        self.w_blk = din("w_blk", [NOG, C, 512], F32)
        self.wp_my = din("wp_my", [HL * 128, C], F32)
        self.a_t = din("a_t", [C, NER], F32)
        self.bf_t = din("bf_t", [NER, C], F32)
        self.bias_in = din("bias_in", [1, C], F32)
        self.coeffs_in = din("coeffs_in", [16, 1], F32)
        self.scales_in = din("scales_in", [16, 1], F32)
        self.sel_in = din("sel_in", [16, 256], F32)
        self.pref_in = din("pref_in", [C, 10], F32)
        self.gating_in = din("gating_in", [1, 1], F32)
        self.cos_in = din("cos_in", [128, T], F32)
        self.sin_in = din("sin_in", [128, T], F32)
        self.ident_in = din("ident_in", [128, 128], BF16)
        self.ones_in = din("ones_in", [128, 128], BF16)
        self.dmask_in = din("dmask_in", [128, 2048], BF16)

        self.out = nc.dram_tensor("out", [TOK, C], F32, kind="ExternalOutput").ap()

        dkind = "ExternalOutput" if DEBUG else "Internal"
        self.y1d = nc.dram_tensor("y1d", [TOK, C], F32, kind=dkind).ap()
        self.y2d = nc.dram_tensor("y2d", [TOK, C], F32, kind=dkind).ap()
        self.z2d_ap = nc.dram_tensor("z2d", [C, TOK], BF16, kind="Internal").ap()
        self.z2g_ap = nc.dram_tensor("z2g", [R * C, TOK], BF16, kind="Internal").ap()
        self.kd_ap = nc.dram_tensor("kd", [HL * 128, GT], BF16, kind=dkind).ap()
        self.vd_ap = nc.dram_tensor("vd", [GT, HL * 128], BF16, kind=dkind).ap()
        self.qdl_ap = nc.dram_tensor("qdl", [HL * 128, GT], BF16, kind=dkind).ap()
        self.yl_ap = nc.dram_tensor("yl", [HL * 128, GT], BF16, kind=dkind).ap()
        self.prtl_ap = [nc.dram_tensor(f"prtl{p}", [GT, 1024], BF16, kind="Internal").ap()
                        for p in range(4)]
        self.rso_ap = [nc.dram_tensor(f"rso{p}", [TOK, 1024], BF16, kind="Internal").ap()
                       for p in range(4)]
        if DEBUG:
            self.rsod = nc.dram_tensor("rsod", [4 * TOK, 1024], BF16,
                                       kind="ExternalOutput").ap()

        e = es.enter_context
        self.SQ = e(nc.psum_tensor("SQ", [128, 2048], F32))
        self.PO = e(nc.psum_tensor("PO", [128, 512], F32))
        self.PL = e(nc.psum_tensor("PL", [128, 512], F32))
        self.PA = e(nc.psum_tensor("PA", [128, 512], F32))
        self.T0 = e(nc.psum_tensor("T0", [128, 1024], BF16))

        self.sS = Sem(es, nc, "sS")
        self.sG = Sem(es, nc, "sG")
        self.sP = Sem(es, nc, "sP")
        self.sA = Sem(es, nc, "sA")
        self.sV = Sem(es, nc, "sV")
        self.sX = Sem(es, nc, "sX")
        self.sCC = Sem(es, nc, "sCC")
        sS, sG, sP, sA, sV = self.sS, self.sG, self.sP, self.sA, self.sV
        W = self.W

        self.ident = e(nc.sbuf_tensor("ident", [128, 128], BF16))
        self.onesb = e(nc.sbuf_tensor("onesb", [128, 128], BF16))
        self.bias_bf = e(nc.sbuf_tensor("bias_bf", [1, C], BF16))
        self.w_er = e(nc.sbuf_tensor("w_er", [128, 2], F32))
        self.coeffs_sb = e(nc.sbuf_tensor("coeffs_sb", [16, 1], F32))
        self.scales_sb = e(nc.sbuf_tensor("scales_sb", [16, 1], F32))
        self.w16 = e(nc.sbuf_tensor("w16", [16, 1], F32))
        self.sel_sb = e(nc.sbuf_tensor("sel_sb", [16, 256], F32))
        self.gating_sb = e(nc.sbuf_tensor("gating_sb", [1, 1], F32))
        self.dmask = e(nc.sbuf_tensor("dmask", [128, 2048], BF16))
        self.akt = e(nc.sbuf_tensor("akt", [128, 40], BF16))
        self.avt = e(nc.sbuf_tensor("avt", [10, 512], BF16))
        self.stats = {k: e(nc.sbuf_tensor(f"st_{k}", [128, 1], F32))
                      for k in ["sum", "ssq", "t1", "tmp", "var", "lnv", "rstd", "negmr"]}

        blk = es.enter_context(nc.Block())
        self.blk = blk

        self.xbuf_free_v = 0
        self.zT_free_pe = 0
        self.t0_free_v = 0
        self.t0_free_a = 0
        self.ln_prev_zbf_pe = [0, 0]
        self.rope_tables_g = 0
        self.pref_loaded_g = 0

        def setup_s(s):
            sS.dinc(s.dma_start(out=self.ident[:], in_=self.ident_in))
            sS.dinc(s.dma_start(out=self.onesb[:], in_=self.ones_in))
            sS.dinc(s.dma_start(out=self.coeffs_sb[:], in_=self.coeffs_in))
            sS.dinc(s.dma_start(out=self.scales_sb[:], in_=self.scales_in))
            sS.dinc(s.dma_start(out=self.sel_sb[:], in_=self.sel_in))
            sS.dinc(s.dma_start(out=self.gating_sb[:], in_=self.gating_in))
            sS.dinc(s.dma_start(out=self.dmask[:], in_=self.dmask_in))
        blk.sync(setup_s)
        self.setup_s_n = sS.n

        def setup_g(g):
            sG.dinc(g.dma_start(out=self.bias_bf[:], in_=self.bias_in))
        blk.gpsimd(setup_g)

        def werv(v):
            W.wait(v, "v", sS, self.setup_s_n)
            sV.inc(v.tensor_tensor(self.w16[:], self.coeffs_sb[:], self.scales_sb[:],
                                   op=ALU.mult))
        blk.vector(werv)
        wer_v = sV.n

        def werm(t):
            W.wait(t, "t", sV, wer_v)
            t.matmul(self.PA[:, 0:1], self.sel_sb[0:16, 0:128], self.w16[:],
                     start=True, stop=True)
            sP.inc(t.matmul(self.PA[:, 1:2], self.sel_sb[0:16, 128:256], self.w16[:],
                            start=True, stop=True))
        blk.tensor(werm)
        wer_p = sP.n

        def werc(v):
            W.wait(v, "v", sP, wer_p)
            sV.inc(v.tensor_copy(self.w_er[:], self.PA[:, 0:2]))
        blk.vector(werc)
        self.wer_ready_v = sV.n

        scA = ExitStack()
        eA = scA.enter_context
        zT = eA(nc.sbuf_tensor("zT", [128, NCC * 1024], BF16))
        self.xbuf = eA(nc.sbuf_tensor("xbuf", [128, C], F32))
        self.zbf = eA(nc.sbuf_tensor("zbf0", [128, C], BF16))
        self.wbuf = [eA(nc.sbuf_tensor(f"wb{i}", [128, 8192], BF16)) for i in range(3)]
        self.xawT = eA(nc.sbuf_tensor("xawT", [128, 2048], BF16))
        self.xabf = eA(nc.sbuf_tensor("xabf", [128, 256], BF16))
        self.xe = [eA(nc.sbuf_tensor(f"xe{i}", [128, 512], F32)) for i in range(2)]
        self.ob = [eA(nc.sbuf_tensor(f"ob{i}", [128, 512], F32)) for i in range(2)]
        self.zT = zT
        self.wbuf_free = [0, 0, 0]
        self.sq_bank_free = [0, 0, 0, 0]
        self.xe_free = [0, 0]
        self.ob_free = [0, 0]

        scA1 = ExitStack()
        eA1 = scA1.enter_context
        AT = eA1(nc.sbuf_tensor("AT", [128, NCC * 256], BF16))
        BfT = eA1(nc.sbuf_tensor("BfT", [128, 2 * C], BF16))
        self.AT, self.BfT = AT, BfT

        def cload(g):
            sG.dinc(g.dma_start(out=AT[:].rearrange("p (cc e) -> p cc e", cc=NCC),
                                in_=self.a_t.rearrange("(cc p) e -> p cc e", p=128)))
            sG.dinc(g.dma_start(out=BfT[:].rearrange("p (ec o) -> p ec o", ec=2),
                                in_=self.bf_t.rearrange("(ec p) o -> p ec o", p=128)))
        blk.gpsimd(cload)
        self.const_loaded_g = sG.n

        self.emit_ln(self.xin, with_xa=True)
        if PHASES == "ALN":
            scA1.close()
            self.emit_z2_gather()
            return self._trunc(scA, es)
        self.emit_mm_stream(self.w_blk, zT, lora=True, biasmm=True, res_scale=0.5,
                            xe_src=self.xin, out_dst=self.y1d)
        ffn1_store_s = self.mm_stream_last_store
        ffn1_pe = self.mm_stream_last_pe
        scA1.close()

        self.zT_free_pe = ffn1_pe
        self.emit_ln(self.y1d, with_xa=False, src_wait_s=ffn1_store_s)
        self.zT_free_pe = 0
        self.emit_z2_gather()
        if PHASES == "A":
            return self._trunc(scA, es)
        scA.close()

        scQ = ExitStack()
        eQ = scQ.enter_context
        self.wqb = [eQ(nc.sbuf_tensor(f"wqb{i}", [128, 4096], BF16)) for i in range(2)]
        self.zgb = [eQ(nc.sbuf_tensor(f"zgb{i}", [128, 16384], BF16)) for i in range(2)]
        self.cosf = eQ(nc.sbuf_tensor("cosf", [128, T], F32))
        self.sinf = eQ(nc.sbuf_tensor("sinf", [128, T], F32))
        self.rt1 = eQ(nc.sbuf_tensor("rt1", [128, 512], F32))
        self.rt2 = eQ(nc.sbuf_tensor("rt2", [128, 512], F32))
        self.robf = [eQ(nc.sbuf_tensor(f"rob{i}", [128, 512], BF16)) for i in range(2)]
        self.vbf = eQ(nc.sbuf_tensor("vbf", [128, 512], BF16))
        self.vtm = eQ(nc.sbuf_tensor("vtm", [128, 512], BF16))
        self.prefT = eQ(nc.sbuf_tensor("prefT", [128, 320], BF16))
        self.aksb = eQ(nc.sbuf_tensor("aksb", [10, 512], BF16))

        def rload(g):
            W.wait(g, "g", sS, self.z2_export_s)
            sG.dinc(g.dma_start(out=self.cosf[:], in_=self.cos_in))
            sG.dinc(g.dma_start(out=self.sinf[:], in_=self.sin_in))
            sG.dinc(g.dma_start(out=self.prefT[:].rearrange("p (cc a) -> p cc a", cc=NCC),
                                in_=self.pref_in.rearrange("(cc p) a -> p cc a", p=128)))
        blk.gpsimd(rload)
        self.rope_tables_g = sG.n
        self.pref_loaded_g = sG.n

        self.emit_qkv_hs()
        self.emit_adapter()
        if PHASES == "Q":
            return self._trunc(scQ, es)
        scQ.close()

        scB = ExitStack()
        eB = scB.enter_context
        self.ktile = [eB(nc.sbuf_tensor(f"kt{i}", [128, 512], BF16)) for i in range(3)]
        self.vtile = [eB(nc.sbuf_tensor(f"vt{i}", [128, 512], BF16)) for i in range(3)]
        self.qtile = [eB(nc.sbuf_tensor(f"qt{i}", [128, 512], BF16)) for i in range(2)]
        self.Etile = [eB(nc.sbuf_tensor(f"E{i}", [128, 2048], BF16)) for i in range(2)]
        self.easb = eB(nc.sbuf_tensor("easb", [10, 512], BF16))
        self.ean_bf = eB(nc.sbuf_tensor("ean_bf", [10, 512], BF16))
        self.lln = eB(nc.sbuf_tensor("lln", [1, 512], F32))
        self.lml = eB(nc.sbuf_tensor("lml", [1, 512], F32))
        self.gdf1 = eB(nc.sbuf_tensor("gdf1", [1, 512], F32))
        self.ga1 = eB(nc.sbuf_tensor("ga1", [1, 512], F32))
        self.ga_bf = eB(nc.sbuf_tensor("ga_bf", [1, 512], BF16))
        self.il_bf = eB(nc.sbuf_tensor("il_bf", [1, 512], BF16))
        self.nb_sb = eB(nc.sbuf_tensor("nb_sb", [128, 512], BF16))
        self.ytile = eB(nc.sbuf_tensor("ytile", [128, 512], BF16))

        self.emit_attention_hs()
        if PHASES == "B":
            return self._trunc(scB, es)
        scB.close()

        scC = ExitStack()
        eC = scC.enter_context
        self.wpb = eC(nc.sbuf_tensor("wpb", [128, HL * C], BF16))
        self.ycb = [eC(nc.sbuf_tensor(f"ycb{i}", [128, HL * 128], BF16)) for i in range(3)]
        self.pob = [eC(nc.sbuf_tensor(f"pob{i}", [128, 512], BF16)) for i in range(2)]
        self.xe = [eC(nc.sbuf_tensor(f"xec{i}", [128, 512], F32)) for i in range(2)]
        self.xe2 = [eC(nc.sbuf_tensor(f"xe2c{i}", [128, 512], F32)) for i in range(2)]
        self.ob = [eC(nc.sbuf_tensor(f"obc{i}", [128, 512], F32)) for i in range(2)]
        self.ob2 = [eC(nc.sbuf_tensor(f"ob2c{i}", [128, 512], F32)) for i in range(2)]
        self.rsb = [eC(nc.sbuf_tensor(f"rsb{i}", [128, 1024], F32)) for i in range(2)]

        self.emit_proj_hs()
        if PHASES == "C":
            return self._trunc(scC, es)
        scC.close()

        scD = ExitStack()
        eD = scD.enter_context
        zTc = eD(nc.sbuf_tensor("zTc", [128, NCC * 1024], BF16))
        self.xbuf = eD(nc.sbuf_tensor("xbufc", [128, C], F32))
        self.zbf = eD(nc.sbuf_tensor("zbc0", [128, C], BF16))
        self.wbuf = [eD(nc.sbuf_tensor(f"wbd{i}", [128, 8192], BF16)) for i in range(3)]
        self.xawT = eD(nc.sbuf_tensor("xawTc", [128, 2048], BF16))
        self.xabf = eD(nc.sbuf_tensor("xabfc", [128, 256], BF16))
        self.xe = [eD(nc.sbuf_tensor(f"xed{i}", [128, 512], F32)) for i in range(2)]
        self.ob = [eD(nc.sbuf_tensor(f"obd{i}", [128, 512], F32)) for i in range(2)]
        ATc = eD(nc.sbuf_tensor("ATc", [128, NCC * 256], BF16))
        BfTc = eD(nc.sbuf_tensor("BfTc", [128, 2 * C], BF16))
        self.zT = zTc
        self.AT, self.BfT = ATc, BfTc
        self.wbuf_free = [0, 0, 0]
        self.sq_bank_free = [0, 0, 0, 0]
        self.xe_free = [0, 0]
        self.ob_free = [0, 0]
        self.ln_prev_zbf_pe = [0, 0]
        self.zT_free_pe = 0
        self.xbuf_free_v = 0

        def cload2(g):
            W.wait(g, "g", sS, self.y2_store_s)
            W.wait(g, "g", sV, self.y2_add_v)
            sG.dinc(g.dma_start(out=ATc[:].rearrange("p (cc e) -> p cc e", cc=NCC),
                                in_=self.a_t.rearrange("(cc p) e -> p cc e", p=128)))
            sG.dinc(g.dma_start(out=BfTc[:].rearrange("p (ec o) -> p ec o", ec=2),
                                in_=self.bf_t.rearrange("(ec p) o -> p ec o", p=128)))
        blk.gpsimd(cload2)
        self.const_loaded_g = sG.n

        self.emit_ln(self.y2d, with_xa=True, src_wait_s=self.y2_store_s)
        self.emit_mm_stream(self.w_blk, zTc, lora=True, biasmm=True, res_scale=0.5,
                            xe_src=self.y2d, out_dst=self.out, xe_wait_s=self.y2_store_s)

        if DEBUG:
            rdb = eD(nc.sbuf_tensor("rdb", [128, 1024], BF16))

            def rdump(s):
                W.wait(s, "sy", sS, self.mm_stream_last_store)
                for p in range(4):
                    for t in range(NT):
                        sS.dinc(s.dma_start(out=rdb[:], in_=self.rso_ap[p][t * 128:(t + 1) * 128, :]))
                        s.wait_ge(sS.h, sS.n)
                        sS.dinc(s.dma_start(
                            out=self.rsod[p * TOK + t * 128: p * TOK + (t + 1) * 128, :],
                            in_=rdb[:]))
                        s.wait_ge(sS.h, sS.n)
            blk.sync(rdump)

        def fin(s):
            s.wait_ge(self.sS.h, self.sS.n)
        blk.sync(fin)
        scD.close()

        self.blk = None
        es.close()
        return nc



def _host_tables():
    theta = (1.0 / (10000.0 ** (np.arange(0, HD, 2, dtype=np.float32) / np.float32(HD)))).astype(np.float32)
    pos = np.arange(T, dtype=np.float32)
    idx = np.outer(pos, theta).astype(np.float32)
    idx2 = np.concatenate([idx, idx], axis=1)
    cos = np.cos(idx2).astype(np.float32).T
    sin = np.sin(idx2).astype(np.float32).T
    ss = sin.copy()
    ss[0:64, :] *= -1.0
    return cos, ss


def _core_slices(r):
    return slice(512 * r, 512 * r + 512), slice(512 * (7 - r), 512 * (7 - r) + 512)


def _make_dmask():
    m = np.zeros((128, 2048), dtype=np.float32)
    k = np.arange(128)[:, None]
    q = np.arange(512)[None, :]
    for c in range(4):
        m[:, c * 512:(c + 1) * 512] = (q >= (c * 128 + k)).astype(np.float32)
    return m.astype(ml_dtypes.bfloat16)


_NC_CACHE = {}


def _get_nc():
    key = ("v2", DEBUG)
    if key not in _NC_CACHE:
        _NC_CACHE[key] = Builder().build()
    return _NC_CACHE[key]


def kernel(L0_attn_x, L0_attn_w_attn, L0_attn_w_proj, L0_attn_adapter_wte, L0_attn_gating,
           L0_ffn_W, L0_ffn_bias, L0_ffn_A, L0_ffn_Bf, L0_ffn_coeffs, L0_ffn_scales):
    x = np.asarray(L0_attn_x, dtype=np.float32)
    w_attn = np.asarray(L0_attn_w_attn, dtype=np.float32)
    w_proj = np.asarray(L0_attn_w_proj, dtype=np.float32)
    awte = np.asarray(L0_attn_adapter_wte, dtype=np.float32)
    gating = np.asarray(L0_attn_gating, dtype=np.float32)
    Wf = np.asarray(L0_ffn_W, dtype=np.float32)
    bias = np.asarray(L0_ffn_bias, dtype=np.float32)
    A = np.asarray(L0_ffn_A, dtype=np.float32)
    Bf = np.asarray(L0_ffn_Bf, dtype=np.float32)
    coeffs = np.asarray(L0_ffn_coeffs, dtype=np.float32)
    scales = np.asarray(L0_ffn_scales, dtype=np.float32)

    w_blk = np.ascontiguousarray(Wf.T.reshape(C, NOG, 512).transpose(1, 0, 2))
    a_t = np.ascontiguousarray(A.transpose(2, 0, 1).reshape(C, NER))
    bf_t = np.ascontiguousarray(Bf.transpose(0, 2, 1).reshape(NER, C))
    pref_t = np.ascontiguousarray(awte.T)
    wT = np.ascontiguousarray(w_attn.T)
    wpT = np.ascontiguousarray(w_proj.T)

    sel = np.zeros((16, 256), dtype=np.float32)
    for ee in range(16):
        sel[ee, ee * 16:(ee + 1) * 16] = 1.0
    cos, ss = _host_tables()
    dmask = _make_dmask()
    ident = np.eye(128, dtype=np.float32).astype(ml_dtypes.bfloat16)
    ones = np.ones((128, 128), dtype=np.float32).astype(ml_dtypes.bfloat16)

    shared = {
        "w_blk": w_blk, "a_t": a_t, "bf_t": bf_t,
        "bias_in": np.ascontiguousarray(bias.reshape(1, C)),
        "coeffs_in": np.ascontiguousarray(coeffs.reshape(16, 1)),
        "scales_in": np.ascontiguousarray(scales.reshape(16, 1)),
        "sel_in": sel, "pref_in": pref_t,
        "gating_in": np.ascontiguousarray(gating.reshape(1, 1)),
        "cos_in": cos, "sin_in": ss,
        "ident_in": ident, "ones_in": ones, "dmask_in": dmask,
    }

    in_maps = []
    for r in range(R):
        s0, s1 = _core_slices(r)
        xin = np.ascontiguousarray(np.concatenate([x[0, s0, :], x[1, s1, :]], axis=0))
        blocks = []
        for p in range(3):
            for i in range(HL):
                h = 4 * r + i
                blocks.append(wT[:, p * C + h * 128: p * C + (h + 1) * 128])
        wqkv = np.ascontiguousarray(np.stack(blocks, axis=0))
        wp_my = np.ascontiguousarray(wpT[4 * r * 128:(4 * r + HL) * 128, :])
        m = dict(shared)
        m.update({"xin": xin, "wqkv_blk": wqkv, "wp_my": wp_my})
        in_maps.append(m)

    nc = _get_nc()
    res = run_bass_kernel_spmd(nc, in_maps, core_ids=list(range(R)))

    out = np.empty((B, T, C), dtype=np.float32)
    for r in range(R):
        s0, s1 = _core_slices(r)
        o = res.results[r]["out"]
        out[0, s0, :] = o[0:512, :]
        out[1, s1, :] = o[512:1024, :]
    if DEBUG:
        kernel._debug_results = res
    return out
